# version 17
# speedup vs baseline: 1.6112x; 1.0370x over previous
import contextlib
import os as _os

import numpy as np
_TRUNC = int(_os.environ.get("MK_TRUNC", "0"))
import ml_dtypes
import concourse.bass as bass
import concourse.mybir as mybir
from concourse.bass_utils import run_bass_kernel_spmd

F32 = mybir.dt.float32
BF16 = mybir.dt.bfloat16
AF = mybir.ActivationFunctionType
ALU = mybir.AluOpType
AX = mybir.AxisListType.X

P = 128
D = 1024
PB = 8
TB = 2048
ST = 8192
DS = 256
NDC = 2
N = 32
DFF = 4096
NJT = 32
TC1 = 512
NT1 = TB // TC1
TCS = 1024
NCH = ST // TCS
NHF = TCS // TC1
UPS = TCS + 4
MAGIC = float(np.float32(1.5 * 2 ** 23))
EPS = 1e-5
GRP4 = [[0, 1, 2, 3], [4, 5, 6, 7]]
GRP8 = [[0, 1, 2, 3, 4, 5, 6, 7]]


class Prog:
    def __init__(self, nc, name):
        self.sem = nc.semaphore(name).__enter__()
        self.n = 0

    def do(self, instr):
        instr.then_inc(self.sem, 1)
        self.n += 1
        return (self.sem, self.n)


class DSem:
    def __init__(self, nc, name):
        self.sem = nc.semaphore(name).__enter__()
        self.n = 0

    def do(self, instr):
        instr.then_inc(self.sem, 16)
        self.n += 16
        return (self.sem, self.n)

    def tok(self):
        return (self.sem, self.n)


class Waiter:
    def __init__(self):
        self.waited = {}

    def __call__(self, q, *toks):
        d = self.waited.setdefault(id(q), {})
        for tok in toks:
            if tok is None:
                continue
            sem, val = tok
            if d.get(id(sem), -1) >= val:
                continue
            q.wait_ge(sem, val)
            d[id(sem)] = val


def build(debug=False):
    nc = bass.Bass("TRN2", target_bir_lowering=False)

    def din(name, shape, dt=F32):
        return nc.dram_tensor(name, shape, dt, kind="ExternalInput").ap()

    x_p = din("x_p", [P, PB * TB])
    wprojT_p = din("wprojT_p", [P, PB * 576])
    woutT_p = din("woutT_p", [P, PB * D])
    wffn1T_p = din("wffn1T_p", [P, NJT * D])
    wffn2_p = din("wffn2_p", [P, NJT * D])
    convw_p = din("convw_p", [P, NDC * 4])
    convb_p = din("convb_p", [P, NDC])
    dparam_p = din("dparam_p", [P, NDC])
    alogT_p = din("alogT_p", [P, NDC * N])
    bin_p = din("bin_p", [P, NJT])
    bout_p = din("bout_p", [P, PB])

    out_fm = nc.dram_tensor("out_fm", [P, PB * TB], F32, kind="ExternalOutput").ap()
    dbg = {}
    if debug:
        for nm, shp, dt in [("dbg_xn", [P, PB * TB], BF16),
                            ("dbg_sc1", [P, 2], F32),
                            ("dbg_sc2", [P, 2], F32),
                            ("dbg_delta", [P, NDC * ST], F32),
                            ("dbg_u", [P, NDC * ST], F32),
                            ("dbg_y", [P, NDC * ST], F32),
                            ("dbg_ffn", [P, PB * TB], BF16)]:
            dbg[nm] = nc.dram_tensor(nm, shp, dt, kind="ExternalOutput").ap()

    def idr(name, shape, dt=F32):
        return nc.dram_tensor(name, shape, dt, kind="Internal").ap()

    ag1_in = [idr(f"ag1_in{h}", [P, PB * (TB // 2)], BF16) for h in range(2)]
    ag1_out = [idr(f"ag1_out{h}", [4, P, PB * (TB // 2)], BF16) for h in range(2)]
    ax1_in = idr("ax1_in", [1, 1])
    ax1_out = idr("ax1_out", [8, 1])
    ax2_in = idr("ax2_in", [1, 1])
    ax2_out = idr("ax2_out", [8, 1])
    ag3_in = [idr(f"ag3_in{h}", [P, NDC * (ST // 2)], BF16) for h in range(2)]
    ag3_out = [idr(f"ag3_out{h}", [4, P, NDC * (ST // 2)], BF16) for h in range(2)]
    amax_dram = idr("amax_dram", [P, 1])
    scal_dram = idr("scal_dram", [1, 4])
    scal2_dram = idr("scal2_dram", [1, 4])
    bc_stage = idr("bc_stage", [64, ST], BF16)
    hid_dram = idr("hid_dram", [P, NJT * TB], BF16)
    w1q_dram = idr("w1q_dram", [P, NJT * D], BF16)
    w2q_dram = idr("w2q_dram", [P, NJT * D], BF16)
    ffn_dram = idr("ffn_dram", [P, PB * TB], BF16)
    qh_dram = idr("qh_dram", [P, NJT * TB], BF16)

    W = Waiter()
    with contextlib.ExitStack() as ES:
        def sb(name, shape, dt=F32):
            return ES.enter_context(nc.sbuf_tensor(name, shape, dt))

        def sem(name):
            return ES.enter_context(nc.semaphore(name))

        TE = Prog(nc, "te_p")
        VE = Prog(nc, "ve_p")
        SC = Prog(nc, "sc_p")
        GP = Prog(nc, "gp_p")

        sm_ld = DSem(nc, "sm_ld")
        sm_w1 = [DSem(nc, f"sm_w1_{i}") for i in range(2)]
        sm_xl = [DSem(nc, f"sm_xl_{i}") for i in range(4)]
        sm_ws = DSem(nc, "sm_ws")
        sm_ax = DSem(nc, "sm_ax")
        sm_axg = DSem(nc, "sm_axg")
        sm_ax2 = DSem(nc, "sm_ax2")
        sm_ax2g = DSem(nc, "sm_ax2g")
        sm_xnf = [DSem(nc, f"sm_xnf_{i}") for i in range(2)]
        sm_rep = [DSem(nc, f"sm_rep_{i}") for i in range(4)]
        sm_bcw = [DSem(nc, f"sm_bcw_{i}") for i in range(2)]
        sm_hidw = [DSem(nc, f"sm_hw_{i}") for i in range(2)]
        sm_qhw = [DSem(nc, f"sm_qhw_{i}") for i in range(2)]
        sm_hidr = [DSem(nc, f"sm_hr_{i}") for i in range(4)]
        sm_w1r = [DSem(nc, f"sm_w1r_{i}") for i in range(4)]
        sm_w2r = [DSem(nc, f"sm_w2r_{i}") for i in range(4)]
        sm_wqw = [DSem(nc, f"sm_wqw_{i}") for i in range(2)]
        sm_yt = [DSem(nc, f"sm_yt_{i}") for i in range(2)]
        sm_x2 = [DSem(nc, f"sm_x2_{i}") for i in range(2)]
        sm_ffnr = [DSem(nc, f"sm_fr_{i}") for i in range(2)]
        sm_yo = DSem(nc, "sm_yo")
        sm_ag = [DSem(nc, f"sm_ag_{i}") for i in range(2)]
        sm_dbg = DSem(nc, "sm_dbg")
        sm_out = [DSem(nc, f"sm_out{i}") for i in range(2)]
        cc1 = [sem("cc1a"), sem("cc1b")]
        ccx1 = sem("ccx1")
        ccx2 = sem("ccx2")
        cc3 = [sem("cc3a"), sem("cc3b")]

        SY = nc.sync
        SQ = nc.scalar
        GQ = nc.gpsimd

        convw = sb("convw", [P, NDC * 4])
        convb = sb("convb", [P, NDC])
        dparam = sb("dparam", [P, NDC])
        A_s = sb("A_s", [P, NDC * N])
        binv = sb("binv", [P, NJT])
        boutv = sb("boutv", [P, PB])
        epi1 = sb("epi1", [P, NJT])
        am_all = sb("am_all", [P, NJT])
        sc1 = sb("sc1", [P, 2])
        sc2 = sb("sc2", [P, 2])
        ident = sb("ident", [P, P], BF16)
        wprojT_bf = sb("wprojT_bf", [P, PB * 576], BF16)
        xbf = sb("xbf", [P, PB * TB], BF16)
        rowtmp = sb("rowtmp", [1, 16])
        amax_acc = sb("amax_acc", [P, 1])
        amax_tmp = sb("amax_tmp", [P, 1])
        arow = sb("arow", [1, P])
        agrow = sb("agrow", [1, 8])
        magic_col = sb("magic_col", [P, 1])
        magic_neg = sb("magic_neg", [P, 1])

        for ap_in, dst in [(convw_p, convw), (convb_p, convb), (dparam_p, dparam),
                           (bin_p, binv), (bout_p, boutv), (alogT_p, A_s)]:
            sm_ld.do(SY.dma_start(out=dst[:], in_=ap_in))

        with contextlib.ExitStack() as ph0:
            def sb0(name, shape, dt=F32):
                return ph0.enter_context(nc.sbuf_tensor(name, shape, dt))

            xt = sb0("xt", [P, PB * TB])
            wstage = sb0("wstage", [P, PB * 576])
            xsq = [sb0(f"xsq{i}", [P, TC1], BF16) for i in range(2)]
            m_t = [sb0(f"m_t{i}", [P, TC1]) for i in range(2)]
            rs_t = [sb0(f"rs_t{i}", [P, TC1]) for i in range(2)]
            v_t = sb0("v_t", [P, TC1])
            msq_t = sb0("msq_t", [P, TC1])
            ones_pp = sb0("ones_pp", [P, P], BF16)
            eps_col = sb0("eps_col", [P, 1])
            iota_t = sb0("iota_t", [P, P])
            ps_ln1 = [ph0.enter_context(nc.psum_tensor(f"ps_ln1{i}", [P, TC1], F32))
                      for i in range(2)]
            ps_ln2 = [ph0.enter_context(nc.psum_tensor(f"ps_ln2{i}", [P, TC1], F32))
                      for i in range(2)]

            t_ld = sm_ld.tok()
            t_xld = []
            for tc in range(NT1):
                c0 = tc * TC1
                t_xld.append(sm_xl[tc].do(SY.dma_start(
                    out=xt.rearrange("p (c t) -> p c t", c=PB)[:, :, c0:c0 + TC1],
                    in_=x_p.rearrange("p (c t) -> p c t", c=PB)[:, :, c0:c0 + TC1])))
            t_ws = sm_ws.do(SY.dma_start(out=wstage[:], in_=wprojT_p))

            t_io = GP.do(GQ.iota(iota_t[:], [[-1, P]], base=0, channel_multiplier=1,
                                 allow_small_or_imprecise_dtypes=True))
            t_mg = VE.do(nc.vector.memset(magic_col[:], MAGIC))
            t_mgn = VE.do(nc.vector.memset(magic_neg[:], -MAGIC))
            t_ep = VE.do(nc.vector.memset(eps_col[:], EPS))
            W(nc.vector, t_io)
            t_id = VE.do(nc.vector.tensor_single_scalar(ident[:], iota_t[:], 0.0, ALU.is_equal))
            t_ones = VE.do(nc.vector.memset(ones_pp[:], 1.0))

            W(nc.scalar, t_ld)
            t_a1 = SC.do(nc.scalar.activation(A_s[:], A_s[:], AF.Exp))
            W(nc.vector, t_a1)
            VE.do(nc.vector.tensor_scalar_mul(A_s[:], A_s[:], -1.0))

            if _TRUNC == 3:
                W(SY, t_ld, t_ws, sm_xl[0].tok(), sm_xl[1].tok(), sm_xl[2].tok(),
                  sm_xl[3].tok(), (VE.sem, VE.n), (SC.sem, SC.n), (GP.sem, GP.n))
                smd = DSem(nc, "sm_trunc")
                for mm_ in range(PB):
                    smd.do(SY.dma_start(out=out_fm[:, mm_ * TB:(mm_ + 1) * TB],
                                        in_=x_p[:, mm_ * TB:(mm_ + 1) * TB]))
                W(SY, smd.tok())
                return nc

            t_ps_free = [None, None]
            t_xn_last = [None] * NT1
            t_rs_prev = None
            for tc in range(NT1):
                p = tc % 2
                c0 = tc * TC1
                mm2tok = []
                t_s1 = None
                for dc in range(PB):
                    sl = slice(dc * TB + c0, dc * TB + c0 + TC1)
                    W(nc.scalar, t_xld[tc])
                    t_cst = SC.do(nc.scalar.copy(xbf[:, sl], xt[:, sl]))
                    if dc >= 2:
                        W(nc.scalar, mm2tok[dc - 2])
                    W(nc.scalar, t_cst)
                    t_sq = SC.do(nc.scalar.activation(xsq[dc % 2][:], xbf[:, sl], AF.Square))
                    W(nc.tensor, t_ones, t_cst, t_ps_free[p])
                    t_s1 = TE.do(nc.tensor.matmul(ps_ln1[p][:, :], ones_pp[:], xbf[:, sl],
                                                  start=(dc == 0), stop=(dc == PB - 1)))
                    W(nc.tensor, t_sq)
                    mm2tok.append(TE.do(nc.tensor.matmul(ps_ln2[p][:, :], ones_pp[:],
                                                         xsq[dc % 2][:],
                                                         start=(dc == 0), stop=(dc == PB - 1))))
                t_s2 = mm2tok[-1]
                W(nc.vector, t_s1, t_s2, t_rs_prev)
                t_m = VE.do(nc.vector.tensor_scalar_mul(m_t[p][:], ps_ln1[p][:, :], 1.0 / D))
                W(nc.vector, t_m)
                t_msq = VE.do(nc.vector.tensor_mul(msq_t[:], m_t[p][:], m_t[p][:]))
                W(nc.vector, t_msq)
                t_v = VE.do(nc.vector.scalar_tensor_tensor(
                    v_t[:], ps_ln2[p][:, :], 1.0 / D, msq_t[:], ALU.mult, ALU.subtract))
                t_ps_free[p] = t_v
                W(nc.scalar, t_v, t_ep, t_rs_prev)
                t_sd = SC.do(nc.scalar.activation(msq_t[:], v_t[:], AF.Sqrt,
                                                  bias=eps_col[:, 0:1]))
                W(nc.vector, t_sd)
                t_rs = VE.do(nc.vector.reciprocal(rs_t[p][:], msq_t[:]))
                t_rs_prev = t_rs
                for dc in range(PB):
                    sl = slice(dc * TB + c0, dc * TB + c0 + TC1)
                    W(nc.vector, t_rs)
                    t_t0 = VE.do(nc.vector.tensor_sub(xt[:, sl], xt[:, sl], m_t[p][:]))
                    W(nc.vector, t_t0)
                    t_xn_last[tc] = VE.do(nc.vector.tensor_mul(xbf[:, sl], xt[:, sl],
                                                               rs_t[p][:]))
                if tc == 1 or tc == 3:
                    hh = tc // 2
                    HB = TB // 2
                    W(SY, t_xn_last[tc - 1], t_xn_last[tc])
                    t_agw = sm_ag[hh].do(SY.dma_start(
                        out=ag1_in[hh],
                        in_=xbf.rearrange("p (c t) -> p c t", c=PB)[:, :, hh * HB:(hh + 1) * HB]))
                    W(GQ, t_agw)
                    GQ.collective_compute("AllGather", ALU.bypass, ins=[ag1_in[hh]],
                                          outs=[ag1_out[hh]],
                                          replica_groups=GRP4).then_inc(cc1[hh], 1)
            W(nc.vector, t_ws)
            t_wp = VE.do(nc.vector.tensor_copy(wprojT_bf[:], wstage[:]))
            ve_bar = (VE.sem, VE.n)
            W(nc.vector, *t_xn_last)
            t_am = VE.do(nc.vector.reduce_max(amax_acc[:], xbf[:], AX,
                                              apply_absolute_value=True))

            if _TRUNC == 4:
                W(SY, (VE.sem, VE.n), (SC.sem, SC.n), (TE.sem, TE.n))
                smd = DSem(nc, "sm_trunc")
                for mm_ in range(PB):
                    smd.do(SY.dma_start(out=out_fm[:, mm_ * TB:(mm_ + 1) * TB],
                                        in_=x_p[:, mm_ * TB:(mm_ + 1) * TB]))
                W(SY, smd.tok())
                return nc

            if debug:
                sm_dbg.do(SY.dma_start(out=dbg["dbg_xn"], in_=xbf[:]))

            W(GQ, t_am)
            t_aw = sm_axg.do(GQ.dma_start(out=amax_dram, in_=amax_acc[:]))
            W(SY, t_aw)
            t_ar = sm_ax.do(SY.dma_start(out=arow[:], in_=amax_dram.rearrange("p one -> one p")))
            W(nc.vector, t_ar)
            t_a0 = VE.do(nc.vector.reduce_max(rowtmp[0:1, 0:1], arow[:], AX))
            W(GQ, t_a0)
            t_axw = sm_axg.do(GQ.dma_start(out=ax1_in, in_=rowtmp[0:1, 0:1]))
            W(GQ, t_axw)
            GQ.collective_compute("AllGather", ALU.bypass, ins=[ax1_in], outs=[ax1_out],
                                  replica_groups=GRP8).then_inc(ccx1, 1)

            if _TRUNC == 5:
                W(SY, (ccx1, 1), (VE.sem, VE.n), (SC.sem, SC.n))
                smd = DSem(nc, "sm_trunc")
                for mm_ in range(PB):
                    smd.do(SY.dma_start(out=out_fm[:, mm_ * TB:(mm_ + 1) * TB],
                                        in_=x_p[:, mm_ * TB:(mm_ + 1) * TB]))
                W(SY, smd.tok())
                return nc

        t_w1q_done = [sm_wqw[0].tok(), sm_wqw[1].tok()]

        with contextlib.ExitStack() as MA:
            def sbm(name, shape, dt=F32):
                return MA.enter_context(nc.sbuf_tensor(name, shape, dt))

            pp = [MA.enter_context(nc.psum_tensor(f"pp{i}", [P, TC1], F32)) for i in range(3)]
            pm = [MA.enter_context(nc.psum_tensor(f"pm{i}", [P, TC1], F32)) for i in range(2)]
            ps_tr = [MA.enter_context(nc.psum_tensor(f"ps_tr{i}", [P, 128], BF16))
                     for i in range(2)]
            wtile_b = [sbm(f"wtile{i}", [P, D]) for i in range(2)]
            wq_f = sbm("wq_f", [P, D])
            wq_bf = sbm("wq_bf", [P, D], BF16)
            wq_bf2 = sbm("wq_bf2", [P, D], BF16)
            sj_col = sbm("sj_col", [P, 1])
            am2_col = sbm("am2_col", [P, 1])
            qh_t = [sbm(f"qh_t{i}", [P, TC1], BF16) for i in range(4)]
            w2s = [sbm(f"w2s{i}", [P, D], BF16) for i in range(4)]
            ffn_sb = [sbm(f"ffn_sb{i}", [P, TC1], BF16) for i in range(2)]
            xnf = [sbm(f"xnf{i}", [P, PB * TCS], BF16) for i in range(2)]
            dlt = [sbm(f"dlt{i}", [P, NDC * TCS]) for i in range(2)]
            sp_t1 = sbm("sp_t1", [P, TC1])
            u_pad = sbm("u_pad", [P, NDC * UPS])
            u_bf = [sbm(f"u_bf{i}", [P, NDC * TCS], BF16) for i in range(2)]
            bc_sb = sbm("bc_sb", [64, TCS], BF16)
            brep = [sbm(f"brep{i}", [P, TCS], BF16) for i in range(4)]
            crep = [sbm(f"crep{i}", [P, TCS], BF16) for i in range(4)]
            a_b = [sbm(f"a_b{i}", [P, NDC * TCS], BF16) for i in range(4)]
            b_b = sbm("b_b", [P, NDC * TCS], BF16)
            h_b = sbm("h_b", [P, NDC * TCS], BF16)
            hc_b = sbm("hc_b", [P, TCS])
            sp_t2 = hc_b[:, 0:TC1]
            y_b = sbm("y_b", [P, NDC * TCS], BF16)
            carry = sbm("carry", [P, NDC * N])
            qx_ch = sbm("qx_ch", [P, PB * TC1], BF16)
            qx_f = sbm("qx_f", [P, TC1])
            w1s = [sbm(f"w1s{i}", [P, D], BF16) for i in range(3)]
            hid_sb = [sbm(f"hid_sb{i}", [P, TC1], BF16) for i in range(2)]
            ham_t = [qx_ch[:, 0:TB], qx_ch[:, TB:2 * TB]]
            hamax_acc = sbm("hamax_acc", [P, 1])
            hamax_tmp = sbm("hamax_tmp", [P, 1])

            for qq in (nc.vector, nc.scalar, nc.tensor, GQ, SY):
                W(qq, sm_xl[0].tok(), sm_xl[1].tok(), sm_xl[2].tok(), sm_xl[3].tok(),
                  sm_ws.tok(), (SC.sem, SC.n), ve_bar, (TE.sem, TE.n))
            t_c0 = VE.do(nc.vector.memset(carry[:], 0.0))
            t_u0 = VE.do(nc.vector.memset(u_pad[:], 0.0))

            t_xnf_rel = [None, None]
            t_rep_rel = [None, None, None, None]
            t_scan_last = {}
            t_bc_w = [None, None]
            bank_ev = [None, None, None]
            pm_ev = [None, None]
            t_yt_prev = None
            t_u_free = {}
            t_upad_free = {}
            t_dlt_free = {}
            t_h_free = {}
            t_yb_free = {}
            t_carry_tok = {}
            t_qx_free = None
            t_ub_prev = None
            t_qxp = None
            t_epi = None
            t_sp_free = None
            t_hc_free = None
            t_sc2 = None
            t_hc_free = None
            t_hid_rel = [None, None]
            t_w1s_rel = [None, None, None, None]
            qst = {}
            w1q_tok = {}
            w2q_tok = {}
            t_ffnw = {}
            t_qh_rel = [None, None, None, None]
            t_w2s_rel = [None, None, None, None]
            t_qhf_rd = None
            t_qhw = {}
            t_mm2qh_rel = [None, None, None, None]
            t_fw_prev = [None, None]
            mm2_ev = {}
            q67_jobs = []
            C_u_done = {}
            C_d_ev = {}
            C_xl = {}
            C_bcw = {}

            def load_xnf(ci):
                pi = ci % 2
                hh = ci % 2
                blkr = ci // 2
                W(SY, t_xnf_rel[pi], (cc1[hh], 1))
                C_xl[ci] = sm_xnf[pi].do(SY.dma_start(
                    out=xnf[pi][:], in_=ag1_out[hh][blkr]))

            def proj_slot(ci, m, hf):
                nonlocal t_sp_free
                par = ci % 2
                slot = m * NHF + hf
                rows = 64 if m == 4 else 128
                bank = pp[slot % 3]
                W(nc.tensor, C_xl[ci], t_wp, bank_ev[slot % 3])
                for k in range(PB):
                    mm = nc.tensor.matmul(
                        bank[0:rows, :],
                        wprojT_bf[:, k * 576 + m * 128:k * 576 + m * 128 + rows],
                        xnf[par][:, k * TCS + hf * TC1:k * TCS + hf * TC1 + TC1],
                        start=(k == 0), stop=(k == PB - 1))
                t_mm = TE.do(mm)
                t_xnf_rel[par] = t_mm
                dc = m % 2
                colh = slice(dc * TCS + hf * TC1, dc * TCS + hf * TC1 + TC1)
                if m < 2:
                    W(nc.scalar, t_mm, t_upad_free.get(dc), t_u0)
                    t_ev = SC.do(nc.scalar.copy(
                        u_pad[:, dc * UPS + 3 + hf * TC1:dc * UPS + 3 + hf * TC1 + TC1],
                        bank[:, :]))
                    C_u_done[("raw", ci, dc, hf)] = t_ev
                elif m < 4:
                    W(nc.scalar, t_mm, t_sp_free)
                    t_p1 = SC.do(nc.scalar.activation(sp_t1[:], bank[:, :], AF.Abs))
                    W(nc.scalar, t_p1)
                    t_p2 = SC.do(nc.scalar.activation(sp_t1[:], sp_t1[:], AF.Exp, scale=-1.0))
                    W(nc.scalar, t_p2)
                    t_p3 = SC.do(nc.scalar.activation(sp_t1[:], sp_t1[:], AF.Ln, bias=1.0))
                    W(nc.scalar, t_ub_prev)
                    t_p4 = SC.do(nc.scalar.activation(sp_t2, bank[:, :], AF.Relu))
                    W(nc.vector, t_p3, t_p4, t_dlt_free.get((par, dc)), t_hc_free)
                    t_ev = VE.do(nc.vector.tensor_add(dlt[par][:, colh], sp_t1[:], sp_t2))
                    t_sp_free = t_ev
                    C_d_ev[(ci, dc, hf)] = t_ev
                else:
                    W(nc.scalar, t_mm, C_bcw.get(ci - 1))
                    t_ev = SC.do(nc.scalar.copy(bc_sb[:, hf * TC1:hf * TC1 + TC1],
                                                bank[0:64, :]))
                    C_d_ev[("bc", ci, hf)] = t_ev
                bank_ev[slot % 3] = t_ev
                return t_ev

            def emit_bcw(ci):
                bpar = ci % 2
                W(SQ, C_d_ev[("bc", ci, 0)], C_d_ev[("bc", ci, 1)], t_bc_w[bpar])
                t = sm_bcw[bpar].do(SQ.dma_start(
                    out=bc_stage[:, ci * TCS:(ci + 1) * TCS], in_=bc_sb[:]))
                t_bc_w[bpar] = t
                C_bcw[ci] = t
                return t

            def emit_conv(ci, dc):
                nonlocal t_ub_prev
                par = ci % 2
                base = dc * UPS
                W(nc.vector, C_u_done[("raw", ci, dc, 0)], C_u_done[("raw", ci, dc, 1)],
                  t_u0, t_hc_free, t_ub_prev,
                  C_d_ev.get((ci, 0, 0)), C_d_ev.get((ci, 0, 1)),
                  C_d_ev.get((ci, 1, 0)), C_d_ev.get((ci, 1, 1)))
                t_cv = VE.do(nc.vector.tensor_scalar(
                    hc_b[:, 0:TCS], u_pad[:, base + 0:base + 0 + TCS],
                    convw[:, dc * 4 + 0:dc * 4 + 1], None, ALU.mult))
                for k in (1, 2, 3):
                    W(nc.vector, t_cv)
                    t_cv = VE.do(nc.vector.scalar_tensor_tensor(
                        hc_b[:, 0:TCS], u_pad[:, base + k:base + k + TCS],
                        convw[:, dc * 4 + k:dc * 4 + k + 1], hc_b[:, 0:TCS],
                        ALU.mult, ALU.add))
                W(nc.vector, t_cv)
                t_tl = VE.do(nc.vector.tensor_copy(u_pad[:, base:base + 3],
                                                   u_pad[:, base + TCS:base + TCS + 3]))
                W(nc.vector, t_tl)
                t_ub = VE.do(nc.vector.tensor_scalar_add(
                    u_pad[:, base + 3:base + 3 + TCS], hc_b[:, 0:TCS],
                    convb[:, dc:dc + 1]))
                W(nc.vector, t_ub, t_u_free.get((par, dc)))
                t_ubf = VE.do(nc.vector.tensor_copy(
                    u_bf[par][:, dc * TCS:(dc + 1) * TCS],
                    u_pad[:, base + 3:base + 3 + TCS]))
                C_u_done[(ci, dc)] = t_ubf
                t_upad_free[dc] = t_ubf
                t_ub_prev = t_ubf
                return t_ubf

            def prep_jobs(ci):
                jobs = []
                for m in range(5):
                    for hf in range(NHF):
                        jobs.append(lambda m=m, hf=hf: proj_slot(ci, m, hf))
                jobs.append(lambda: emit_bcw(ci))
                for dc in range(NDC):
                    jobs.append(lambda dc=dc: emit_conv(ci, dc))
                return jobs

            def quant_tile(kind, jt):
                src_ap, dst_dram = ((wffn1T_p, w1q_dram) if kind == 1
                                    else (wffn2_p, w2q_dram))
                par = jt % 2
                wtile = wtile_b[par]
                W(SY, qst.get("wt_amx%d" % par), qst.get("wt_ri%d" % par))
                t_l = sm_w1[par].do(SY.dma_start(out=wtile[:],
                                                 in_=src_ap[:, jt * D:(jt + 1) * D]))
                W(nc.vector, t_l)
                if kind == 1:
                    t_amx = VE.do(nc.vector.reduce_max(am_all[:, jt:jt + 1], wtile[:], AX,
                                                       apply_absolute_value=True))
                    am_ap = am_all[:, jt:jt + 1]
                else:
                    W(nc.vector, qst.get("am2_rd"))
                    t_amx = VE.do(nc.vector.reduce_max(am2_col[:], wtile[:], AX,
                                                       apply_absolute_value=True))
                    am_ap = am2_col[:]
                qst["wt_amx"] = t_amx
                qst["wt_amx%d" % par] = t_amx
                W(nc.vector, t_amx, qst.get("sj_rd"))
                t_rc = VE.do(nc.vector.reciprocal(sj_col[:], am_ap))
                W(nc.vector, t_rc)
                t_sj = VE.do(nc.vector.tensor_scalar_mul(sj_col[:], sj_col[:], 127.0))
                W(nc.scalar, t_sj, t_l, t_mg, qst.get("wqf_rd"))
                t_ri = SC.do(nc.scalar.activation(wq_f[:], wtile[:], AF.Identity,
                                                  scale=sj_col[:, 0:1], bias=magic_col[:, 0:1]))
                qst["wt_ri%d" % par] = t_ri
                qst["sj_rd"] = t_ri
                if kind == 1:
                    W(nc.vector, t_ri, qst.get("wqbf_rd"))
                    t_q = VE.do(nc.vector.tensor_scalar_add(wq_bf[:], wq_f[:], -MAGIC))
                    qst["wqf_rd"] = t_q
                    evtoks = []
                    for k in range(PB):
                        half = k % 2
                        W(nc.tensor, t_q, t_id)
                        if k >= 2:
                            W(nc.tensor, evtoks[k - 2])
                        t_tr = TE.do(nc.tensor.transpose(
                            ps_tr[half][:], wq_bf[:, k * 128:(k + 1) * 128], ident[:]))
                        W(nc.scalar, t_tr, qst.get("wqbf2_rd"))
                        evtoks.append(SC.do(nc.scalar.copy(
                            wq_bf2[:, k * 128:(k + 1) * 128], ps_tr[half][:])))
                    qst["wqbf_rd"] = t_tr
                    W(GQ, *evtoks)
                    t_w = sm_wqw[par].do(GQ.dma_start(
                        out=dst_dram[:, jt * D:(jt + 1) * D], in_=wq_bf2[:]))
                    qst["wqbf2_rd"] = t_w
                    qst["w1fin%d" % par] = t_w
                    w1q_tok[jt] = t_w
                else:
                    W(nc.vector, t_ri)
                    t_iv = VE.do(nc.vector.tensor_scalar_mul(sj_col[:], am_ap, 1.0 / 127.0))
                    qst["am2_rd"] = t_iv
                    qst["sj_rd"] = t_iv
                    W(nc.vector, t_iv, qst.get("wqbf_rd"))
                    t_q = VE.do(nc.vector.tensor_scalar(wq_f[:], wq_f[:], -MAGIC,
                                                        sj_col[:, 0:1], ALU.add, ALU.mult))
                    qst["wqf_rd"] = t_q
                    W(nc.vector, t_q, qst.get("wqbf_rd"))
                    t_c = VE.do(nc.vector.tensor_copy(wq_bf[:], wq_f[:]))
                    W(GQ, t_c, qst.get("w1fin%d" % par))
                    t_w = sm_wqw[par].do(GQ.dma_start(
                        out=dst_dram[:, jt * D:(jt + 1) * D], in_=wq_bf[:]))
                    qst["wqbf_rd"] = t_w
                    w2q_tok[jt] = t_w

            def mm1_qx(tc1):
                nonlocal t_qxp, t_qx_free
                c1 = tc1 * TC1
                toks = {}
                for k in range(PB):
                    W(nc.scalar, t_sc1, (ccx1, 1), t_mg, t_mgn, t_qxp, t_qx_free)
                    t_r1 = SC.do(nc.scalar.activation(
                        qx_f[:], xbf[:, k * TB + c1:k * TB + c1 + TC1], AF.Identity,
                        scale=sc1[:, 0:1], bias=magic_col[:, 0:1]))
                    W(nc.scalar, t_r1)
                    toks[k] = SC.do(nc.scalar.activation(
                        qx_ch[:, k * TC1:(k + 1) * TC1], qx_f[:], AF.Identity,
                        bias=magic_neg[:, 0:1]))
                    t_qxp = toks[k]
                return toks

            def mm1_job(tc1, m, t_qx):
                nonlocal t_qx_free, t_epi
                c1 = tc1 * TC1
                mpar = m % 2
                m4 = m % 3
                W(SY, t_w1s_rel[m4], w1q_tok[m])
                t_wl = sm_w1r[m % 4].do(SY.dma_start(
                    out=w1s[m4][:], in_=w1q_dram[:, m * D:(m + 1) * D]))
                bank = pm[mpar]
                W(nc.tensor, t_wl, pm_ev[mpar], *t_qx.values())
                for k in range(PB):
                    mm = nc.tensor.matmul(bank[:, :], w1s[m4][:, k * 128:(k + 1) * 128],
                                          qx_ch[:, k * TC1:(k + 1) * TC1],
                                          start=(k == 0), stop=(k == PB - 1))
                t_mmf = TE.do(mm)
                t_w1s_rel[m4] = t_mmf
                t_qx_free = t_mmf
                W(nc.scalar, t_mmf, t_hid_rel[mpar], t_epi)
                t_hev = SC.do(nc.scalar.activation(
                    hid_sb[mpar][:], bank[:, :], AF.Gelu,
                    scale=epi1[:, m:m + 1], bias=binv[:, m:m + 1]))
                pm_ev[mpar] = t_hev
                W(SQ, t_hev)
                t_hw = sm_hidw[mpar].do(SQ.dma_start(
                    out=hid_dram[:, m * TB + c1:m * TB + c1 + TC1], in_=hid_sb[mpar][:]))
                t_hid_rel[mpar] = t_hw

            prep_tok = {}
            t_hidw_q = [None, None]

            def qh_prep_unit(j):
                nonlocal t_qhf_rd
                tc2, k = divmod(j, NJT)
                u = k * NT1 + tc2
                kpar = j % 4
                wpar = j % 2
                W(SY, t_qh_rel[kpar], t_hidw_q[0], t_hidw_q[1])
                t_hl = sm_hidr[kpar].do(SY.dma_start(
                    out=qh_t[kpar][:], in_=hid_dram[:, u * TC1:(u + 1) * TC1]))
                W(nc.scalar, t_hl, t_sc2, (ccx2, 1), t_mg, t_mgn, t_qhf_rd, t_qxp, t_qx_free)
                t_r1 = SC.do(nc.scalar.activation(
                    qx_f[:], qh_t[kpar][:], AF.Identity,
                    scale=sc2[:, 0:1], bias=magic_col[:, 0:1]))
                W(nc.scalar, t_r1, t_qhw.get(wpar))
                t_qh = SC.do(nc.scalar.activation(
                    hid_sb[wpar][:], qx_f[:], AF.Identity, bias=magic_neg[:, 0:1]))
                t_qhf_rd = t_qh
                W(SQ, t_qh)
                t_qhw[wpar] = sm_qhw[wpar].do(SQ.dma_start(
                    out=qh_dram[:, u * TC1:(u + 1) * TC1], in_=hid_sb[wpar][:]))
                prep_tok[(k, tc2)] = t_qhw[wpar]
                t_qh_rel[kpar] = t_r1

            def mm2_unit(u):
                halfm, rem = divmod(u, NT1 * NJT)
                tc2, k = divmod(rem, NJT)
                c1 = tc2 * TC1
                kpar = k % 4
                banks4 = [pp[0], pp[1], pm[0], pm[1]]
                W(SY, t_w2s_rel[kpar], w2q_tok[k])
                t_w2l = sm_w2r[kpar].do(SY.dma_start(
                    out=w2s[kpar][:, 0:512],
                    in_=w2q_dram[:, k * D + halfm * 512:k * D + halfm * 512 + 512]))
                W(SY, t_qh_rel[kpar], prep_tok[(k, tc2)])
                t_hl = sm_hidr[kpar].do(SY.dma_start(
                    out=qh_t[kpar][:], in_=qh_dram[:, (k * NT1 + tc2) * TC1:(k * NT1 + tc2 + 1) * TC1]))
                t_qh = t_hl
                W(nc.tensor, t_w2l, t_qh)
                if k == 0:
                    W(nc.tensor, *[mm2_ev.get(i) for i in range(4)], *bank_ev, *pm_ev)
                t_mm_last = None
                for mi in range(4):
                    m = halfm * 4 + mi
                    mm = nc.tensor.matmul(
                        banks4[mi][:, :], w2s[kpar][:, mi * 128:(mi + 1) * 128],
                        qh_t[kpar][:], start=(k == 0), stop=(k == NJT - 1))
                t_mm_last = TE.do(mm)
                t_w2s_rel[kpar] = t_mm_last
                t_qh_rel[kpar] = t_mm_last
                if k == NJT - 1:
                    for mi in range(4):
                        m = halfm * 4 + mi
                        fp = m % 2
                        W(nc.scalar, t_mm_last, t_fw_prev[fp])
                        t_fe = SC.do(nc.scalar.activation(
                            ffn_sb[fp][:], banks4[mi][:, :], AF.Identity,
                            scale=sc2[:, 1:2], bias=boutv[:, m:m + 1]))
                        mm2_ev[mi] = t_fe
                        W(SQ, t_fe)
                        t_fw_prev[fp] = sm_hidw[fp].do(SQ.dma_start(
                            out=ffn_dram[:, m * TB + c1:m * TB + c1 + TC1],
                            in_=ffn_sb[fp][:]))
                        t_ffnw[(tc2, m)] = t_fw_prev[fp]

            for jt_ in range(32):
                quant_tile(1, jt_)
            load_xnf(0)
            for job in prep_jobs(0):
                job()

            quant_sched = {2: [(2, j) for j in range(16)],
                           3: [(2, j) for j in range(16, 32)]}
            MM2_CH = NCH - 1
            N_MM2 = 2 * NT1 * NJT

            for ch in range(NCH):
                par = ch % 2
                c0g = ch * TCS
                if ch == NCH // 2:
                    W(GQ, sm_yt[0].tok(), sm_yt[1].tok())
                    GQ.collective_compute("AllGather", ALU.bypass, ins=[ag3_in[0]],
                                          outs=[ag3_out[0]],
                                          replica_groups=GRP4).then_inc(cc3[0], 1)
                if ch + 1 < NCH:
                    load_xnf(ch + 1)
                jobs = prep_jobs(ch + 1) if ch + 1 < NCH else []
                if 1 <= ch < 1 + NT1:
                    tc1 = ch - 1
                    qx_box = {}

                    def qx_job(tc1=tc1, box=qx_box):
                        box.update(mm1_qx(tc1))

                    mm1jobs = [qx_job] + [
                        (lambda m=m, tc1=tc1, box=qx_box: mm1_job(tc1, m, box))
                        for m in range(NJT)]
                    mixed = []
                    ia = ib = 0
                    while ia < len(jobs) or ib < len(mm1jobs):
                        if ia < len(jobs):
                            mixed.append(jobs[ia]); ia += 1
                        for _ in range(3):
                            if ib < len(mm1jobs):
                                mixed.append(mm1jobs[ib]); ib += 1
                    jobs = mixed
                jobs_at = {}
                for j, job in enumerate(jobs):
                    jobs_at.setdefault(1 + j, []).append(job)
                if ch == 1:
                    W(SY, (ccx1, 1))
                    t_agr = sm_ax.do(SY.dma_start(out=agrow[:], in_=ax1_out.rearrange("r one -> one r")))
                    W(nc.vector, t_agr)
                    t_g0 = VE.do(nc.vector.reduce_max(rowtmp[0:1, 1:2], agrow[:], AX))
                    W(nc.vector, t_g0)
                    t_g1 = VE.do(nc.vector.reciprocal(rowtmp[0:1, 2:3], rowtmp[0:1, 1:2]))
                    W(nc.vector, t_g1)
                    t_g2 = VE.do(nc.vector.tensor_scalar_mul(rowtmp[0:1, 2:3], rowtmp[0:1, 2:3], 32767.0))
                    t_g3 = VE.do(nc.vector.tensor_scalar_mul(rowtmp[0:1, 3:4], rowtmp[0:1, 1:2], 1.0 / 32767.0))
                    W(SY, t_g2, t_g3)
                    t_sw = sm_ax.do(SY.dma_start(out=scal_dram[0:1, 0:2], in_=rowtmp[0:1, 2:4]))
                    W(SY, t_sw)
                    t_sc1 = sm_ax.do(SY.dma_start(out=sc1[:], in_=scal_dram[0:1, 0:2].broadcast_to([P, 2])))
                    W(nc.vector, t_sc1, (ccx1, 1), qst["wt_amx"])
                    t_epi = VE.do(nc.vector.tensor_scalar(epi1[:], am_all[:], sc1[:, 1:2],
                                                          1.0 / 127.0, ALU.mult, ALU.mult))
                if ch == 6:
                    W(SY, (ccx2, 1))
                    t_agr2 = sm_ax2.do(SY.dma_start(out=agrow[:], in_=ax2_out.rearrange("r one -> one r")))
                    W(nc.vector, t_agr2)
                    t_b1 = VE.do(nc.vector.reduce_max(rowtmp[0:1, 5:6], agrow[:], AX))
                    W(nc.vector, t_b1)
                    t_b2 = VE.do(nc.vector.reciprocal(rowtmp[0:1, 6:7], rowtmp[0:1, 5:6]))
                    W(nc.vector, t_b2)
                    t_b3 = VE.do(nc.vector.tensor_scalar_mul(rowtmp[0:1, 6:7], rowtmp[0:1, 6:7], 32767.0))
                    t_b4 = VE.do(nc.vector.tensor_scalar_mul(rowtmp[0:1, 7:8], rowtmp[0:1, 5:6], 1.0 / 32767.0))
                    W(SY, t_b3, t_b4)
                    t_sw2 = sm_ax2.do(SY.dma_start(out=scal2_dram[0:1, 0:2], in_=rowtmp[0:1, 6:8]))
                    W(SY, t_sw2)
                    t_sc2 = sm_ax2.do(SY.dma_start(out=sc2[:], in_=scal2_dram[0:1, 0:2].broadcast_to([P, 2])))
                    if debug:
                        W(SY, t_sc2)
                        sm_dbg.do(SY.dma_start(out=dbg["dbg_sc2"], in_=sc2[:]))
                if ch == MM2_CH - 1 and not q67_jobs:
                    seq = []
                    for tc2 in range(NT1):
                        seq += [("p", tc2 * NJT + k) for k in range(NJT)]
                        if tc2 >= 1:
                            seq += [("m", (tc2 - 1) * NJT + k) for k in range(NJT)]
                    seq += [("m", 3 * NJT + k) for k in range(NJT)]
                    seq += [("m", NT1 * NJT + uu) for uu in range(NT1 * NJT)]
                    q67_jobs.extend(seq)

                t_rep_tok = {}
                t_aa_tok = {}

                def issue_rep(nn):
                    q = nn % 4
                    W(SY, t_rep_rel[q], C_bcw[ch])
                    sm_rep[q].do(SY.dma_start(
                        out=brep[q][:],
                        in_=bc_stage[nn:nn + 1, c0g:c0g + TCS].broadcast_to([P, TCS])))
                    t_rep_tok[q] = sm_rep[q].do(SY.dma_start(
                        out=crep[q][:],
                        in_=bc_stage[32 + nn:33 + nn, c0g:c0g + TCS].broadcast_to([P, TCS])))

                def issue_exp(nn):
                    q = nn % 4
                    for dc_ in range(NDC):
                        dsl_ = slice(dc_ * TCS, (dc_ + 1) * TCS)
                        W(nc.scalar, C_d_ev[(ch, dc_, 0)], C_d_ev[(ch, dc_, 1)],
                          t_scan_last.get((q, dc_)))
                        t_aa_tok[(q, dc_)] = SC.do(nc.scalar.activation(
                            a_b[q][:, dsl_], dlt[par][:, dsl_], AF.Exp,
                            scale=A_s[:, dc_ * N + nn:dc_ * N + nn + 1]))
                        if nn == N - 1:
                            t_dlt_free[(par, dc_)] = t_aa_tok[(q, dc_)]

                issue_rep(0)
                issue_rep(1)
                issue_exp(0)
                issue_exp(1)
                for n in range(N):
                    q4 = n % 4
                    if n + 2 < N:
                        issue_rep(n + 2)
                        issue_exp(n + 2)
                    brep_v = brep[q4][:].rearrange("p (o t) -> p o t", o=1).broadcast_to([P, NDC, TCS])
                    crep_v = crep[q4][:].rearrange("p (o t) -> p o t", o=1).broadcast_to([P, NDC, TCS])
                    bb_v = b_b[:].rearrange("p (c t) -> p c t", c=NDC)
                    hb_v = h_b[:].rearrange("p (c t) -> p c t", c=NDC)
                    yb_v = y_b[:].rearrange("p (c t) -> p c t", c=NDC)
                    ub_v = u_bf[par][:].rearrange("p (c t) -> p c t", c=NDC)
                    W(nc.vector, t_rep_tok[q4], C_u_done[(ch, 0)], C_u_done[(ch, 1)],
                      t_scan_last.get((q4, 0)), t_scan_last.get((q4, 1)),
                      t_scan_last.get(((n - 1) % 4, 0)), t_scan_last.get(((n - 1) % 4, 1)),
                      t_h_free.get(0), t_h_free.get(1))
                    t_bb = VE.do(nc.vector.tensor_tensor(bb_v, brep_v, ub_v, ALU.mult))
                    for dc in range(NDC):
                        ci = n * NDC + dc
                        dsl = slice(dc * TCS, (dc + 1) * TCS)
                        t_aa = t_aa_tok[(q4, dc)]
                        W(nc.vector, t_aa, t_bb, t_c0,
                          t_h_free.get(dc), t_carry_tok.get((n, dc)))
                        init = 0.0 if ch == 0 else carry[:, ci:ci + 1]
                        t_h = VE.do(nc.vector.tensor_tensor_scan(
                            h_b[:, dsl], a_b[q4][:, dsl], b_b[:, dsl],
                            init, ALU.mult, ALU.add))
                        t_scan_last[(q4, dc)] = t_h
                    W(nc.vector, t_scan_last[(q4, 0)], t_scan_last[(q4, 1)])
                    t_cc = VE.do(nc.vector.tensor_copy(
                        carry[:, 2 * n:2 * n + 2].rearrange("p (c o) -> p c o", c=NDC),
                        hb_v[:, :, TCS - 1:TCS]))
                    t_carry_tok[(n, 0)] = t_cc
                    t_carry_tok[(n, 1)] = t_cc
                    if n == 0:
                        W(nc.vector, t_yb_free.get(0), t_yb_free.get(1), t_yt_prev)
                        t_y = VE.do(nc.vector.tensor_tensor(yb_v, crep_v, hb_v, ALU.mult))
                    else:
                        t_hc = VE.do(nc.vector.tensor_tensor(bb_v, crep_v, hb_v, ALU.mult))
                        W(nc.vector, t_hc)
                        t_y = VE.do(nc.vector.tensor_add(y_b[:], y_b[:], b_b[:]))
                    t_h_free[0] = t_y
                    t_h_free[1] = t_y
                    t_rep_rel[q4] = t_y
                    if ch >= MM2_CH - 1:
                        it67 = (ch - (MM2_CH - 1)) * 2 * N + n * NDC
                        budget = 8 if it67 >= 18 else (4 if ch == MM2_CH - 1 else 8)
                        while budget > 0 and q67_jobs:
                            knd, uu = q67_jobs.pop(0)
                            (qh_prep_unit if knd == "p" else mm2_unit)(uu)
                            budget -= 1
                    for job in jobs_at.pop(n, []):
                        job()
                for nn in sorted(jobs_at):
                    for job in jobs_at[nn]:
                        job()
                while ch == MM2_CH and q67_jobs:
                    knd, uu = q67_jobs.pop(0)
                    (qh_prep_unit if knd == "p" else mm2_unit)(uu)

                for dc in range(NDC):
                    dsl = slice(dc * TCS, (dc + 1) * TCS)
                    W(nc.vector, t_h_free[dc], t_yt_prev)
                    t_ytd = VE.do(nc.vector.scalar_tensor_tensor(
                        y_b[:, dsl], u_bf[par][:, dsl],
                        dparam[:, dc:dc + 1], y_b[:, dsl], ALU.mult, ALU.add))
                    t_u_free[(par, dc)] = t_ytd
                    t_yb_free[dc] = t_ytd
                    if debug:
                        W(SY, t_ytd)
                        sm_dbg.do(SY.dma_start(out=dbg["dbg_y"][:, dc * ST + c0g:dc * ST + c0g + TCS],
                                               in_=y_b[:, dsl]))
                    W(GQ, t_ytd)
                    hh3 = c0g // (ST // 2)
                    off3 = c0g % (ST // 2)
                    sm_yt[par].do(GQ.dma_start(
                        out=ag3_in[hh3][:, dc * (ST // 2) + off3:dc * (ST // 2) + off3 + TCS],
                        in_=y_b[:, dsl]))
                t_yt_prev = sm_yt[par].tok()
                if debug:
                    W(SY, C_d_ev[(ch, 0, 1)], C_d_ev[(ch, 1, 1)], C_d_ev[(ch, 0, 0)], C_d_ev[(ch, 1, 0)])
                    for dc in range(NDC):
                        sm_dbg.do(SY.dma_start(out=dbg["dbg_delta"][:, dc * ST + c0g:dc * ST + c0g + TCS],
                                               in_=dlt[par][:, dc * TCS:(dc + 1) * TCS]))
                        sm_dbg.do(SY.dma_start(out=dbg["dbg_u"][:, dc * ST + c0g:dc * ST + c0g + TCS],
                                               in_=u_pad[:, dc * UPS + 3:dc * UPS + 3 + TCS]))

                for kind, jt in quant_sched.get(ch, []):
                    quant_tile(kind, jt)

                if ch == 4:
                    t_hh = [sm_hidw[0].tok(), sm_hidw[1].tok()]
                    t_hidw_q[0], t_hidw_q[1] = t_hh
                    t_hx = None
                    for mi in range(NJT):
                        hpar = mi % 2
                        W(SY, *t_hh)
                        if mi >= 2:
                            W(SY, t_hx)
                        if mi < 2:
                            W(SY, t_qx_free)
                        t_hl = sm_hidr[mi % 2].do(SY.dma_start(
                            out=ham_t[hpar], in_=hid_dram[:, mi * TB:(mi + 1) * TB]))
                        W(nc.vector, t_hl)
                        if mi == 0:
                            t_hx = VE.do(nc.vector.reduce_max(hamax_acc[:], ham_t[hpar], AX,
                                                              apply_absolute_value=True))
                        else:
                            t_q = VE.do(nc.vector.reduce_max(hamax_tmp[:], ham_t[hpar], AX,
                                                             apply_absolute_value=True))
                            W(nc.vector, t_q)
                            t_hx = VE.do(nc.vector.tensor_max(hamax_acc[:], hamax_acc[:], hamax_tmp[:]))
                    W(GQ, t_hx)
                    t_aw2 = sm_ax2g.do(GQ.dma_start(out=amax_dram, in_=hamax_acc[:]))
                    W(SY, t_aw2)
                    t_ar2 = sm_ax2.do(SY.dma_start(out=arow[:], in_=amax_dram.rearrange("p one -> one p")))
                    W(nc.vector, t_ar2)
                    t_b0 = VE.do(nc.vector.reduce_max(rowtmp[0:1, 4:5], arow[:], AX))
                    W(GQ, t_b0)
                    t_axw2 = sm_ax2g.do(GQ.dma_start(out=ax2_in, in_=rowtmp[0:1, 4:5]))
                    W(GQ, t_axw2)
                    GQ.collective_compute("AllGather", ALU.bypass, ins=[ax2_in], outs=[ax2_out],
                                          replica_groups=GRP8).then_inc(ccx2, 1)


            W(GQ, sm_yt[0].tok(), sm_yt[1].tok())
            GQ.collective_compute("AllGather", ALU.bypass, ins=[ag3_in[1]], outs=[ag3_out[1]],
                                  replica_groups=GRP4).then_inc(cc3[1], 1)

        if _TRUNC == 2:
            W(SY, (cc3[0], 1), (cc3[1], 1), (ccx2, 1), (VE.sem, VE.n), (SC.sem, SC.n), (TE.sem, TE.n))
            smd = DSem(nc, "sm_trunc")
            for mm_ in range(PB):
                smd.do(SY.dma_start(out=out_fm[:, mm_ * TB:(mm_ + 1) * TB],
                                    in_=x_p[:, mm_ * TB:(mm_ + 1) * TB]))
            W(SY, smd.tok())
            return nc

        with contextlib.ExitStack() as TL:
            def sbt(name, shape, dt=F32):
                return TL.enter_context(nc.sbuf_tensor(name, shape, dt))

            pq = [TL.enter_context(nc.psum_tensor(f"pq{i}", [P, TC1], F32)) for i in range(8)]
            ffn2_t = [sbt(f"ffn2_t{i}", [P, TC1], BF16) for i in range(2)]
            woutT_bf = sbt("woutT_bf", [P, PB * D], BF16)
            wostage = sbt("wostage", [P, PB * D])
            y_own = sbt("y_own", [P, PB * TB], BF16)
            x2_t = [sbt(f"x2_t{i}", [P, TC1]) for i in range(2)]
            at_t = [sbt(f"at_t{i}", [P, TC1]) for i in range(2)]
            o_t = [sbt(f"o_t{i}", [P, TC1]) for i in range(2)]

            for qq in (nc.vector, nc.scalar, nc.tensor, GQ, SY):
                W(qq, (SC.sem, SC.n), (VE.sem, VE.n), (TE.sem, TE.n),
                  sm_yt[0].tok(), sm_yt[1].tok(), sm_xnf[0].tok(), sm_xnf[1].tok(),
                  sm_rep[0].tok(), sm_rep[1].tok(), sm_bcw[0].tok(), sm_bcw[1].tok(),
                  sm_hidw[0].tok(), sm_hidw[1].tok(), sm_qhw[0].tok(), sm_qhw[1].tok(),
                  sm_hidr[0].tok(), sm_hidr[1].tok(),
                  sm_w1r[0].tok(), sm_w1r[1].tok(), sm_wqw[0].tok(), sm_wqw[1].tok(),
                  sm_w1[0].tok(), sm_w1[1].tok())
            t_wo = sm_ld.do(SY.dma_start(out=wostage[:], in_=woutT_p))
            W(nc.vector, t_wo)
            t_wob = VE.do(nc.vector.tensor_copy(woutT_bf[:], wostage[:]))

            pq_ev = [None] * 8
            if debug:
                W(SY, *[t_ffnw[(t2, m2)] for t2 in range(NT1) for m2 in range(PB)])
                sm_dbg.do(SY.dma_start(out=dbg["dbg_ffn"], in_=ffn_dram))

            for arm in SY.switch_core_id(n=128):
                blk = arm.logical % 4
                h3 = blk // 2
                blk2 = blk % 2
                SY.wait_ge(cc3[h3], 1)
                for r in range(4):
                    for c in range(NDC):
                        g = r * NDC + c
                        SY.dma_start(
                            out=y_own[:, g * TB:(g + 1) * TB],
                            in_=ag3_out[h3][r, :, c * (ST // 2) + blk2 * TB:c * (ST // 2) + (blk2 + 1) * TB]
                        ).then_inc(sm_yo.sem, 16)
            sm_yo.n += PB * 16
            t_yo = sm_yo.tok()

            t_o2_prev = [None, None]
            t_st_prev = [None, None]
            it = 0
            for tc in range(NT1):
                c1 = tc * TC1
                for m in range(PB):
                    ip = it % 2
                    bank = pq[m]
                    W(nc.tensor, t_yo, t_wob, pq_ev[m])
                    for k in range(PB):
                        mm = nc.tensor.matmul(bank[:, :],
                                              woutT_bf[:, k * D + m * 128:k * D + m * 128 + 128],
                                              y_own[:, k * TB + c1:k * TB + c1 + TC1],
                                              start=(k == 0), stop=(k == PB - 1))
                    t_mm_o = TE.do(mm)
                    W(SY, t_o2_prev[ip])
                    t_xl2 = sm_x2[ip].do(SY.dma_start(
                        out=x2_t[ip][:], in_=x_p[:, m * TB + c1:m * TB + c1 + TC1]))
                    W(SY, t_ffnw[(tc, m)], t_o2_prev[ip])
                    t_fl2 = sm_ffnr[ip].do(SY.dma_start(
                        out=ffn2_t[ip][:], in_=ffn_dram[:, m * TB + c1:m * TB + c1 + TC1]))
                    W(nc.scalar, t_mm_o, t_o2_prev[ip])
                    t_ae = SC.do(nc.scalar.copy(at_t[ip][:], bank[:, :]))
                    pq_ev[m] = t_ae
                    W(nc.vector, t_ae, t_xl2, t_fl2, t_st_prev[ip])
                    t_o1 = VE.do(nc.vector.tensor_add(o_t[ip][:], at_t[ip][:], x2_t[ip][:]))
                    W(nc.vector, t_o1)
                    t_o2 = VE.do(nc.vector.tensor_add(o_t[ip][:], o_t[ip][:], ffn2_t[ip][:]))
                    t_o2_prev[ip] = t_o2
                    W(SY, t_o2)
                    t_st_prev[ip] = sm_out[ip].do(SY.dma_start(
                        out=out_fm[:, m * TB + c1:m * TB + c1 + TC1], in_=o_t[ip][:]))
                    it += 1

            W(SY, sm_out[0].tok(), sm_out[1].tok())
            if debug:
                W(SY, sm_dbg.tok())

    return nc


def _fm(a):
    r = a.shape[0] // P
    return np.ascontiguousarray(a.reshape(r, P, a.shape[1]).transpose(1, 0, 2).reshape(P, -1))


def kernel(**inputs):
    x = np.asarray(inputs["L0_attn_x"])
    W_in = np.asarray(inputs["L0_attn_W_in"])
    W_out = np.asarray(inputs["L0_attn_W_out"])
    conv_w = np.asarray(inputs["L0_attn_conv_w"])
    conv_b = np.asarray(inputs["L0_attn_conv_b"])
    A_log = np.asarray(inputs["L0_attn_A_log"])
    D_param = np.asarray(inputs["L0_attn_D_param"])
    Wf1 = np.asarray(inputs["L0_ffn_W_in"])
    bf1 = np.asarray(inputs["L0_ffn_b_in"])
    Wf2 = np.asarray(inputs["L0_ffn_W_out"])
    bf2 = np.asarray(inputs["L0_ffn_b_out"])

    x_fm = np.ascontiguousarray(np.transpose(x, (0, 2, 1)))
    woutT = _fm(np.ascontiguousarray(W_out.T))
    wffn1T = _fm(np.ascontiguousarray(Wf1.T))
    wffn2 = _fm(np.ascontiguousarray(Wf2))
    bin_p = np.ascontiguousarray(bf1.reshape(NJT, P).T)
    bout_p = np.ascontiguousarray(bf2.reshape(PB, P).T)

    in_maps = []
    for c in range(8):
        s, q = c // 4, c % 4
        sl = slice(DS * q, DS * (q + 1))
        wproj = np.concatenate([W_in[DS * q:DS * (q + 1)],
                                W_in[D + DS * q:D + DS * (q + 1)],
                                W_in[2 * D:2 * D + 64]], axis=0)
        in_maps.append({
            "x_p": _fm(np.ascontiguousarray(x_fm[s, :, TB * q:TB * (q + 1)])),
            "wprojT_p": _fm(np.ascontiguousarray(wproj.T)),
            "woutT_p": woutT,
            "wffn1T_p": wffn1T,
            "wffn2_p": wffn2,
            "convw_p": _fm(np.ascontiguousarray(conv_w[sl, 0, :])),
            "convb_p": _fm(conv_b[sl].reshape(DS, 1)),
            "dparam_p": _fm(D_param[sl].reshape(DS, 1)),
            "alogT_p": _fm(np.ascontiguousarray(A_log.T[sl])),
            "bin_p": bin_p,
            "bout_p": bout_p,
        })

    nc = build(debug=False)
    res = run_bass_kernel_spmd(nc, in_maps, core_ids=list(range(8)))

    out = np.empty((2, ST, D), np.float32)
    for c in range(8):
        s, q = c // 4, c % 4
        o = np.asarray(res.results[c]["out_fm"])
        o = o.reshape(P, PB, TB).transpose(1, 0, 2).reshape(D, TB)
        out[s, TB * q:TB * (q + 1), :] = o.T
    return out


# version 19 (final)
# speedup vs baseline: 1.6268x; 1.0097x over previous
import contextlib
import os as _os

import numpy as np
_TRUNC = int(_os.environ.get("MK_TRUNC", "0"))
import ml_dtypes
import concourse.bass as bass
import concourse.mybir as mybir
from concourse.bass_utils import run_bass_kernel_spmd

F32 = mybir.dt.float32
BF16 = mybir.dt.bfloat16
AF = mybir.ActivationFunctionType
ALU = mybir.AluOpType
AX = mybir.AxisListType.X

P = 128
D = 1024
PB = 8
TB = 2048
ST = 8192
DS = 256
NDC = 2
N = 32
DFF = 4096
NJT = 32
TC1 = 512
NT1 = TB // TC1
TCS = 1024
NCH = ST // TCS
NHF = TCS // TC1
UPS = TCS + 4
MAGIC = float(np.float32(1.5 * 2 ** 23))
EPS = 1e-5
GRP4 = [[0, 1, 2, 3], [4, 5, 6, 7]]
GRP8 = [[0, 1, 2, 3, 4, 5, 6, 7]]


class Prog:
    def __init__(self, nc, name):
        self.sem = nc.semaphore(name).__enter__()
        self.n = 0

    def do(self, instr):
        instr.then_inc(self.sem, 1)
        self.n += 1
        return (self.sem, self.n)


class DSem:
    def __init__(self, nc, name):
        self.sem = nc.semaphore(name).__enter__()
        self.n = 0

    def do(self, instr):
        instr.then_inc(self.sem, 16)
        self.n += 16
        return (self.sem, self.n)

    def tok(self):
        return (self.sem, self.n)


class Waiter:
    def __init__(self):
        self.waited = {}

    def __call__(self, q, *toks):
        d = self.waited.setdefault(id(q), {})
        for tok in toks:
            if tok is None:
                continue
            sem, val = tok
            if d.get(id(sem), -1) >= val:
                continue
            q.wait_ge(sem, val)
            d[id(sem)] = val


def build(debug=False):
    nc = bass.Bass("TRN2", target_bir_lowering=False)

    def din(name, shape, dt=F32):
        return nc.dram_tensor(name, shape, dt, kind="ExternalInput").ap()

    x_p = din("x_p", [P, PB * TB])
    wprojT_p = din("wprojT_p", [P, PB * 576])
    woutT_p = din("woutT_p", [P, PB * D])
    wffn1T_p = din("wffn1T_p", [P, NJT * D])
    wffn2_p = din("wffn2_p", [P, NJT * D])
    convw_p = din("convw_p", [P, NDC * 4])
    convb_p = din("convb_p", [P, NDC])
    dparam_p = din("dparam_p", [P, NDC])
    alogT_p = din("alogT_p", [P, NDC * N])
    bin_p = din("bin_p", [P, NJT])
    bout_p = din("bout_p", [P, PB])

    out_fm = nc.dram_tensor("out_fm", [P, PB * TB], F32, kind="ExternalOutput").ap()
    dbg = {}
    if debug:
        for nm, shp, dt in [("dbg_xn", [P, PB * TB], BF16),
                            ("dbg_sc1", [P, 2], F32),
                            ("dbg_sc2", [P, 2], F32),
                            ("dbg_delta", [P, NDC * ST], F32),
                            ("dbg_u", [P, NDC * ST], F32),
                            ("dbg_y", [P, NDC * ST], F32),
                            ("dbg_ffn", [P, PB * TB], BF16)]:
            dbg[nm] = nc.dram_tensor(nm, shp, dt, kind="ExternalOutput").ap()

    def idr(name, shape, dt=F32):
        return nc.dram_tensor(name, shape, dt, kind="Internal").ap()

    ag1_in = [idr(f"ag1_in{h}", [P, PB * (TB // 2)], BF16) for h in range(2)]
    ag1_out = [idr(f"ag1_out{h}", [4, P, PB * (TB // 2)], BF16) for h in range(2)]
    ax1_in = idr("ax1_in", [1, 1])
    ax1_out = idr("ax1_out", [8, 1])
    ax2_in = idr("ax2_in", [1, 1])
    ax2_out = idr("ax2_out", [8, 1])
    AG3W = [ST // 2, ST // 4, ST // 8, ST // 8]
    AG3S = [0, ST // 2, 3 * ST // 4, 7 * ST // 8]
    ag3_in = [idr(f"ag3_in{i}", [P, NDC * AG3W[i]], BF16) for i in range(4)]
    ag3_out = [idr(f"ag3_out{i}", [4, P, NDC * AG3W[i]], BF16) for i in range(4)]
    amax_dram = idr("amax_dram", [P, 1])
    scal_dram = idr("scal_dram", [1, 4])
    scal2_dram = idr("scal2_dram", [1, 4])
    bc_stage = idr("bc_stage", [64, ST], BF16)
    hid_dram = idr("hid_dram", [P, NJT * TB], BF16)
    w1q_dram = idr("w1q_dram", [P, NJT * D], BF16)
    w2q_dram = idr("w2q_dram", [P, NJT * D], BF16)
    ffn_dram = idr("ffn_dram", [P, PB * TB], BF16)
    qh_dram = idr("qh_dram", [P, NJT * TB], BF16)

    W = Waiter()
    with contextlib.ExitStack() as ES:
        def sb(name, shape, dt=F32):
            return ES.enter_context(nc.sbuf_tensor(name, shape, dt))

        def sem(name):
            return ES.enter_context(nc.semaphore(name))

        TE = Prog(nc, "te_p")
        VE = Prog(nc, "ve_p")
        SC = Prog(nc, "sc_p")
        GP = Prog(nc, "gp_p")

        sm_ld = DSem(nc, "sm_ld")
        sm_w1 = [DSem(nc, f"sm_w1_{i}") for i in range(2)]
        sm_xl = [DSem(nc, f"sm_xl_{i}") for i in range(4)]
        sm_ws = DSem(nc, "sm_ws")
        sm_ax = DSem(nc, "sm_ax")
        sm_axg = DSem(nc, "sm_axg")
        sm_ax2 = DSem(nc, "sm_ax2")
        sm_ax2g = DSem(nc, "sm_ax2g")
        sm_xnf = [DSem(nc, f"sm_xnf_{i}") for i in range(2)]
        sm_rep = [DSem(nc, f"sm_rep_{i}") for i in range(4)]
        sm_bcw = [DSem(nc, f"sm_bcw_{i}") for i in range(2)]
        sm_hidw = [DSem(nc, f"sm_hw_{i}") for i in range(2)]
        sm_qhw = [DSem(nc, f"sm_qhw_{i}") for i in range(2)]
        sm_hidr = [DSem(nc, f"sm_hr_{i}") for i in range(4)]
        sm_w1r = [DSem(nc, f"sm_w1r_{i}") for i in range(4)]
        sm_w2r = [DSem(nc, f"sm_w2r_{i}") for i in range(4)]
        sm_wqw = [DSem(nc, f"sm_wqw_{i}") for i in range(2)]
        sm_yt = [DSem(nc, f"sm_yt_{i}") for i in range(2)]
        sm_x2 = [DSem(nc, f"sm_x2_{i}") for i in range(2)]
        sm_ffnr = [DSem(nc, f"sm_fr_{i}") for i in range(2)]
        sm_yo = DSem(nc, "sm_yo")
        sm_yog = [DSem(nc, f"sm_yog{i}") for i in range(8)]
        sm_yog2 = [DSem(nc, f"sm_yogb{i}") for i in range(8)]
        sm_ag = [DSem(nc, f"sm_ag_{i}") for i in range(2)]
        sm_dbg = DSem(nc, "sm_dbg")
        sm_out = [DSem(nc, f"sm_out{i}") for i in range(2)]
        cc1 = [sem("cc1a"), sem("cc1b")]
        ccx1 = sem("ccx1")
        ccx2 = sem("ccx2")
        cc3 = [sem("cc3a"), sem("cc3b"), sem("cc3c"), sem("cc3d")]

        SY = nc.sync
        SQ = nc.scalar
        GQ = nc.gpsimd

        convw = sb("convw", [P, NDC * 4])
        convb = sb("convb", [P, NDC])
        dparam = sb("dparam", [P, NDC])
        A_s = sb("A_s", [P, NDC * N])
        binv = sb("binv", [P, NJT])
        boutv = sb("boutv", [P, PB])
        epi1 = sb("epi1", [P, NJT])
        am_all = sb("am_all", [P, NJT])
        sc1 = sb("sc1", [P, 2])
        sc2 = sb("sc2", [P, 2])
        ident = sb("ident", [P, P], BF16)
        wprojT_bf = sb("wprojT_bf", [P, PB * 576], BF16)
        xbf = sb("xbf", [P, PB * TB], BF16)
        rowtmp = sb("rowtmp", [1, 16])
        amax_acc = sb("amax_acc", [P, 1])
        amax_tmp = sb("amax_tmp", [P, 1])
        arow = sb("arow", [1, P])
        agrow = sb("agrow", [1, 8])
        magic_col = sb("magic_col", [P, 1])
        magic_neg = sb("magic_neg", [P, 1])

        for ap_in, dst in [(convw_p, convw), (convb_p, convb), (dparam_p, dparam),
                           (bin_p, binv), (bout_p, boutv), (alogT_p, A_s)]:
            sm_ld.do(SY.dma_start(out=dst[:], in_=ap_in))

        with contextlib.ExitStack() as ph0:
            def sb0(name, shape, dt=F32):
                return ph0.enter_context(nc.sbuf_tensor(name, shape, dt))

            xt = sb0("xt", [P, PB * TB])
            wstage = sb0("wstage", [P, PB * 576])
            xsq = [sb0(f"xsq{i}", [P, TC1], BF16) for i in range(2)]
            m_t = [sb0(f"m_t{i}", [P, TC1]) for i in range(2)]
            rs_t = [sb0(f"rs_t{i}", [P, TC1]) for i in range(2)]
            v_t = sb0("v_t", [P, TC1])
            msq_t = sb0("msq_t", [P, TC1])
            ones_pp = sb0("ones_pp", [P, P], BF16)
            eps_col = sb0("eps_col", [P, 1])
            iota_t = sb0("iota_t", [P, P])
            ps_ln1 = [ph0.enter_context(nc.psum_tensor(f"ps_ln1{i}", [P, TC1], F32))
                      for i in range(2)]
            ps_ln2 = [ph0.enter_context(nc.psum_tensor(f"ps_ln2{i}", [P, TC1], F32))
                      for i in range(2)]

            t_ld = sm_ld.tok()
            t_xld = []
            for tc in range(NT1):
                c0 = tc * TC1
                t_xld.append(sm_xl[tc].do(SY.dma_start(
                    out=xt.rearrange("p (c t) -> p c t", c=PB)[:, :, c0:c0 + TC1],
                    in_=x_p.rearrange("p (c t) -> p c t", c=PB)[:, :, c0:c0 + TC1])))
            t_ws = sm_ws.do(SY.dma_start(out=wstage[:], in_=wprojT_p))

            t_io = GP.do(GQ.iota(iota_t[:], [[-1, P]], base=0, channel_multiplier=1,
                                 allow_small_or_imprecise_dtypes=True))
            t_mg = VE.do(nc.vector.memset(magic_col[:], MAGIC))
            t_mgn = VE.do(nc.vector.memset(magic_neg[:], -MAGIC))
            t_ep = VE.do(nc.vector.memset(eps_col[:], EPS))
            W(nc.vector, t_io)
            t_id = VE.do(nc.vector.tensor_single_scalar(ident[:], iota_t[:], 0.0, ALU.is_equal))
            t_ones = VE.do(nc.vector.memset(ones_pp[:], 1.0))

            W(nc.scalar, t_ld)
            t_a1 = SC.do(nc.scalar.activation(A_s[:], A_s[:], AF.Exp))
            W(nc.vector, t_a1)
            VE.do(nc.vector.tensor_scalar_mul(A_s[:], A_s[:], -1.0))

            if _TRUNC == 3:
                W(SY, t_ld, t_ws, sm_xl[0].tok(), sm_xl[1].tok(), sm_xl[2].tok(),
                  sm_xl[3].tok(), (VE.sem, VE.n), (SC.sem, SC.n), (GP.sem, GP.n))
                smd = DSem(nc, "sm_trunc")
                for mm_ in range(PB):
                    smd.do(SY.dma_start(out=out_fm[:, mm_ * TB:(mm_ + 1) * TB],
                                        in_=x_p[:, mm_ * TB:(mm_ + 1) * TB]))
                W(SY, smd.tok())
                return nc

            t_ps_free = [None, None]
            t_xn_last = [None] * NT1
            t_rs_prev = None
            for tc in range(NT1):
                p = tc % 2
                c0 = tc * TC1
                mm2tok = []
                t_s1 = None
                for dc in range(PB):
                    sl = slice(dc * TB + c0, dc * TB + c0 + TC1)
                    W(nc.scalar, t_xld[tc])
                    t_cst = SC.do(nc.scalar.copy(xbf[:, sl], xt[:, sl]))
                    if dc >= 2:
                        W(nc.scalar, mm2tok[dc - 2])
                    W(nc.scalar, t_cst)
                    t_sq = SC.do(nc.scalar.activation(xsq[dc % 2][:], xbf[:, sl], AF.Square))
                    W(nc.tensor, t_ones, t_cst, t_ps_free[p])
                    t_s1 = TE.do(nc.tensor.matmul(ps_ln1[p][:, :], ones_pp[:], xbf[:, sl],
                                                  start=(dc == 0), stop=(dc == PB - 1)))
                    W(nc.tensor, t_sq)
                    mm2tok.append(TE.do(nc.tensor.matmul(ps_ln2[p][:, :], ones_pp[:],
                                                         xsq[dc % 2][:],
                                                         start=(dc == 0), stop=(dc == PB - 1))))
                t_s2 = mm2tok[-1]
                W(nc.vector, t_s1, t_s2, t_rs_prev)
                t_m = VE.do(nc.vector.tensor_scalar_mul(m_t[p][:], ps_ln1[p][:, :], 1.0 / D))
                W(nc.vector, t_m)
                t_msq = VE.do(nc.vector.tensor_mul(msq_t[:], m_t[p][:], m_t[p][:]))
                W(nc.vector, t_msq)
                t_v = VE.do(nc.vector.scalar_tensor_tensor(
                    v_t[:], ps_ln2[p][:, :], 1.0 / D, msq_t[:], ALU.mult, ALU.subtract))
                t_ps_free[p] = t_v
                W(nc.scalar, t_v, t_ep, t_rs_prev)
                t_sd = SC.do(nc.scalar.activation(msq_t[:], v_t[:], AF.Sqrt,
                                                  bias=eps_col[:, 0:1]))
                W(nc.vector, t_sd)
                t_rs = VE.do(nc.vector.reciprocal(rs_t[p][:], msq_t[:]))
                t_rs_prev = t_rs
                for dc in range(PB):
                    sl = slice(dc * TB + c0, dc * TB + c0 + TC1)
                    W(nc.vector, t_rs)
                    t_t0 = VE.do(nc.vector.tensor_sub(xt[:, sl], xt[:, sl], m_t[p][:]))
                    W(nc.vector, t_t0)
                    t_xn_last[tc] = VE.do(nc.vector.tensor_mul(xbf[:, sl], xt[:, sl],
                                                               rs_t[p][:]))
                if tc == 1 or tc == 3:
                    hh = tc // 2
                    HB = TB // 2
                    W(SY, t_xn_last[tc - 1], t_xn_last[tc])
                    t_agw = sm_ag[hh].do(SY.dma_start(
                        out=ag1_in[hh],
                        in_=xbf.rearrange("p (c t) -> p c t", c=PB)[:, :, hh * HB:(hh + 1) * HB]))
                    W(GQ, t_agw)
                    GQ.collective_compute("AllGather", ALU.bypass, ins=[ag1_in[hh]],
                                          outs=[ag1_out[hh]],
                                          replica_groups=GRP4).then_inc(cc1[hh], 1)
            W(nc.vector, t_ws)
            t_wp = VE.do(nc.vector.tensor_copy(wprojT_bf[:], wstage[:]))
            ve_bar = (VE.sem, VE.n)
            W(nc.vector, *t_xn_last)
            t_am = VE.do(nc.vector.reduce_max(amax_acc[:], xbf[:], AX,
                                              apply_absolute_value=True))

            if _TRUNC == 4:
                W(SY, (VE.sem, VE.n), (SC.sem, SC.n), (TE.sem, TE.n))
                smd = DSem(nc, "sm_trunc")
                for mm_ in range(PB):
                    smd.do(SY.dma_start(out=out_fm[:, mm_ * TB:(mm_ + 1) * TB],
                                        in_=x_p[:, mm_ * TB:(mm_ + 1) * TB]))
                W(SY, smd.tok())
                return nc

            if debug:
                sm_dbg.do(SY.dma_start(out=dbg["dbg_xn"], in_=xbf[:]))

            W(GQ, t_am)
            t_aw = sm_axg.do(GQ.dma_start(out=amax_dram, in_=amax_acc[:]))
            W(SY, t_aw)
            t_ar = sm_ax.do(SY.dma_start(out=arow[:], in_=amax_dram.rearrange("p one -> one p")))
            W(nc.vector, t_ar)
            t_a0 = VE.do(nc.vector.reduce_max(rowtmp[0:1, 0:1], arow[:], AX))
            W(GQ, t_a0)
            t_axw = sm_axg.do(GQ.dma_start(out=ax1_in, in_=rowtmp[0:1, 0:1]))
            W(GQ, t_axw)
            GQ.collective_compute("AllGather", ALU.bypass, ins=[ax1_in], outs=[ax1_out],
                                  replica_groups=GRP8).then_inc(ccx1, 1)

            if _TRUNC == 5:
                W(SY, (ccx1, 1), (VE.sem, VE.n), (SC.sem, SC.n))
                smd = DSem(nc, "sm_trunc")
                for mm_ in range(PB):
                    smd.do(SY.dma_start(out=out_fm[:, mm_ * TB:(mm_ + 1) * TB],
                                        in_=x_p[:, mm_ * TB:(mm_ + 1) * TB]))
                W(SY, smd.tok())
                return nc

        t_w1q_done = [sm_wqw[0].tok(), sm_wqw[1].tok()]

        with contextlib.ExitStack() as MA:
            def sbm(name, shape, dt=F32):
                return MA.enter_context(nc.sbuf_tensor(name, shape, dt))

            pp = [MA.enter_context(nc.psum_tensor(f"pp{i}", [P, TC1], F32)) for i in range(3)]
            pm = [MA.enter_context(nc.psum_tensor(f"pm{i}", [P, TC1], F32)) for i in range(2)]
            ps_tr = [MA.enter_context(nc.psum_tensor(f"ps_tr{i}", [P, 128], BF16))
                     for i in range(2)]
            wtile_b = [sbm(f"wtile{i}", [P, D]) for i in range(2)]
            wq_f = sbm("wq_f", [P, D])
            wq_bf = sbm("wq_bf", [P, D], BF16)
            wq_bf2 = sbm("wq_bf2", [P, D], BF16)
            sj_col = sbm("sj_col", [P, 1])
            am2_col = sbm("am2_col", [P, 1])
            qh_t = [sbm(f"qh_t{i}", [P, TC1], BF16) for i in range(4)]
            w2s = [sbm(f"w2s{i}", [P, D], BF16) for i in range(4)]
            ffn_sb = [sbm(f"ffn_sb{i}", [P, TC1], BF16) for i in range(2)]
            xnf = [sbm(f"xnf{i}", [P, PB * TCS], BF16) for i in range(2)]
            dlt = [sbm(f"dlt{i}", [P, NDC * TCS]) for i in range(2)]
            sp_t1 = sbm("sp_t1", [P, TC1])
            u_pad = sbm("u_pad", [P, NDC * UPS])
            u_bf = [sbm(f"u_bf{i}", [P, NDC * TCS], BF16) for i in range(2)]
            bc_sb = sbm("bc_sb", [64, TCS], BF16)
            brep = [sbm(f"brep{i}", [P, TCS], BF16) for i in range(4)]
            crep = [sbm(f"crep{i}", [P, TCS], BF16) for i in range(4)]
            a_b = [sbm(f"a_b{i}", [P, NDC * TCS], BF16) for i in range(4)]
            b_b = sbm("b_b", [P, NDC * TCS], BF16)
            h_b = sbm("h_b", [P, NDC * TCS], BF16)
            hc_b = sbm("hc_b", [P, TCS])
            sp_t2 = hc_b[:, 0:TC1]
            y_b = sbm("y_b", [P, NDC * TCS], BF16)
            carry = sbm("carry", [P, NDC * N])
            qx_ch = sbm("qx_ch", [P, PB * TC1], BF16)
            qx_f = sbm("qx_f", [P, TC1])
            w1s = [sbm(f"w1s{i}", [P, D], BF16) for i in range(3)]
            hid_sb = [sbm(f"hid_sb{i}", [P, TC1], BF16) for i in range(2)]
            ham_t = [qx_ch[:, 0:TB], qx_ch[:, TB:2 * TB]]
            hamax_acc = sbm("hamax_acc", [P, 1])
            hamax_tmp = sbm("hamax_tmp", [P, 1])

            for qq in (nc.vector, nc.scalar, nc.tensor, GQ, SY):
                W(qq, sm_xl[0].tok(), sm_xl[1].tok(), sm_xl[2].tok(), sm_xl[3].tok(),
                  sm_ws.tok(), (SC.sem, SC.n), ve_bar, (TE.sem, TE.n))
            t_c0 = VE.do(nc.vector.memset(carry[:], 0.0))
            t_u0 = VE.do(nc.vector.memset(u_pad[:], 0.0))

            t_xnf_rel = [None, None]
            t_rep_rel = [None, None, None, None]
            t_scan_last = {}
            t_bc_w = [None, None]
            bank_ev = [None, None, None]
            pm_ev = [None, None]
            t_yt_prev = None
            t_u_free = {}
            t_upad_free = {}
            t_dlt_free = {}
            t_h_free = {}
            t_yb_free = {}
            t_carry_tok = {}
            t_qx_free = None
            t_ub_prev = None
            t_qxp = None
            t_epi = None
            t_sp_free = None
            t_hc_free = None
            t_sc2 = None
            t_hc_free = None
            t_hid_rel = [None, None]
            t_w1s_rel = [None, None, None, None]
            qst = {}
            w1q_tok = {}
            w2q_tok = {}
            t_ffnw = {}
            t_qh_rel = [None, None, None, None]
            t_w2s_rel = [None, None, None, None]
            t_qhf_rd = None
            t_qhw = {}
            t_mm2qh_rel = [None, None, None, None]
            t_fw_prev = [None, None]
            mm2_ev = {}
            q67_jobs = []
            C_u_done = {}
            C_d_ev = {}
            C_xl = {}
            C_bcw = {}

            def load_xnf(ci):
                pi = ci % 2
                hh = ci % 2
                blkr = ci // 2
                W(SY, t_xnf_rel[pi], (cc1[hh], 1))
                C_xl[ci] = sm_xnf[pi].do(SY.dma_start(
                    out=xnf[pi][:], in_=ag1_out[hh][blkr]))

            def proj_slot(ci, m, hf):
                nonlocal t_sp_free
                par = ci % 2
                slot = m * NHF + hf
                rows = 64 if m == 4 else 128
                bank = pp[slot % 3]
                W(nc.tensor, C_xl[ci], t_wp, bank_ev[slot % 3])
                for k in range(PB):
                    mm = nc.tensor.matmul(
                        bank[0:rows, :],
                        wprojT_bf[:, k * 576 + m * 128:k * 576 + m * 128 + rows],
                        xnf[par][:, k * TCS + hf * TC1:k * TCS + hf * TC1 + TC1],
                        start=(k == 0), stop=(k == PB - 1))
                t_mm = TE.do(mm)
                t_xnf_rel[par] = t_mm
                dc = m % 2
                colh = slice(dc * TCS + hf * TC1, dc * TCS + hf * TC1 + TC1)
                if m < 2:
                    W(nc.scalar, t_mm, t_upad_free.get(dc), t_u0)
                    t_ev = SC.do(nc.scalar.copy(
                        u_pad[:, dc * UPS + 3 + hf * TC1:dc * UPS + 3 + hf * TC1 + TC1],
                        bank[:, :]))
                    C_u_done[("raw", ci, dc, hf)] = t_ev
                elif m < 4:
                    W(nc.scalar, t_mm, t_sp_free)
                    t_p1 = SC.do(nc.scalar.activation(sp_t1[:], bank[:, :], AF.Abs))
                    W(nc.scalar, t_p1)
                    t_p2 = SC.do(nc.scalar.activation(sp_t1[:], sp_t1[:], AF.Exp, scale=-1.0))
                    W(nc.scalar, t_p2)
                    t_p3 = SC.do(nc.scalar.activation(sp_t1[:], sp_t1[:], AF.Ln, bias=1.0))
                    W(nc.scalar, t_ub_prev)
                    t_p4 = SC.do(nc.scalar.activation(sp_t2, bank[:, :], AF.Relu))
                    W(nc.vector, t_p3, t_p4, t_dlt_free.get((par, dc)), t_hc_free)
                    t_ev = VE.do(nc.vector.tensor_add(dlt[par][:, colh], sp_t1[:], sp_t2))
                    t_sp_free = t_ev
                    C_d_ev[(ci, dc, hf)] = t_ev
                else:
                    W(nc.scalar, t_mm, C_bcw.get(ci - 1))
                    t_ev = SC.do(nc.scalar.copy(bc_sb[:, hf * TC1:hf * TC1 + TC1],
                                                bank[0:64, :]))
                    C_d_ev[("bc", ci, hf)] = t_ev
                bank_ev[slot % 3] = t_ev
                return t_ev

            def emit_bcw(ci):
                bpar = ci % 2
                W(SQ, C_d_ev[("bc", ci, 0)], C_d_ev[("bc", ci, 1)], t_bc_w[bpar])
                t = sm_bcw[bpar].do(SQ.dma_start(
                    out=bc_stage[:, ci * TCS:(ci + 1) * TCS], in_=bc_sb[:]))
                t_bc_w[bpar] = t
                C_bcw[ci] = t
                return t

            def emit_conv(ci, dc):
                nonlocal t_ub_prev
                par = ci % 2
                base = dc * UPS
                W(nc.vector, C_u_done[("raw", ci, dc, 0)], C_u_done[("raw", ci, dc, 1)],
                  t_u0, t_hc_free, t_ub_prev,
                  C_d_ev.get((ci, 0, 0)), C_d_ev.get((ci, 0, 1)),
                  C_d_ev.get((ci, 1, 0)), C_d_ev.get((ci, 1, 1)))
                t_cv = VE.do(nc.vector.tensor_scalar(
                    hc_b[:, 0:TCS], u_pad[:, base + 0:base + 0 + TCS],
                    convw[:, dc * 4 + 0:dc * 4 + 1], None, ALU.mult))
                for k in (1, 2, 3):
                    W(nc.vector, t_cv)
                    t_cv = VE.do(nc.vector.scalar_tensor_tensor(
                        hc_b[:, 0:TCS], u_pad[:, base + k:base + k + TCS],
                        convw[:, dc * 4 + k:dc * 4 + k + 1], hc_b[:, 0:TCS],
                        ALU.mult, ALU.add))
                W(nc.vector, t_cv)
                t_tl = VE.do(nc.vector.tensor_copy(u_pad[:, base:base + 3],
                                                   u_pad[:, base + TCS:base + TCS + 3]))
                W(nc.vector, t_tl)
                t_ub = VE.do(nc.vector.tensor_scalar_add(
                    u_pad[:, base + 3:base + 3 + TCS], hc_b[:, 0:TCS],
                    convb[:, dc:dc + 1]))
                W(nc.vector, t_ub, t_u_free.get((par, dc)))
                t_ubf = VE.do(nc.vector.tensor_copy(
                    u_bf[par][:, dc * TCS:(dc + 1) * TCS],
                    u_pad[:, base + 3:base + 3 + TCS]))
                C_u_done[(ci, dc)] = t_ubf
                t_upad_free[dc] = t_ubf
                t_ub_prev = t_ubf
                return t_ubf

            def prep_jobs(ci):
                jobs = []
                for m in range(5):
                    for hf in range(NHF):
                        jobs.append(lambda m=m, hf=hf: proj_slot(ci, m, hf))
                jobs.append(lambda: emit_bcw(ci))
                for dc in range(NDC):
                    jobs.append(lambda dc=dc: emit_conv(ci, dc))
                return jobs

            def quant_tile(kind, jt):
                src_ap, dst_dram = ((wffn1T_p, w1q_dram) if kind == 1
                                    else (wffn2_p, w2q_dram))
                par = jt % 2
                wtile = wtile_b[par]
                W(SY, qst.get("wt_amx%d" % par), qst.get("wt_ri%d" % par))
                t_l = sm_w1[par].do(SY.dma_start(out=wtile[:],
                                                 in_=src_ap[:, jt * D:(jt + 1) * D]))
                W(nc.vector, t_l)
                if kind == 1:
                    t_amx = VE.do(nc.vector.reduce_max(am_all[:, jt:jt + 1], wtile[:], AX,
                                                       apply_absolute_value=True))
                    am_ap = am_all[:, jt:jt + 1]
                else:
                    W(nc.vector, qst.get("am2_rd"))
                    t_amx = VE.do(nc.vector.reduce_max(am2_col[:], wtile[:], AX,
                                                       apply_absolute_value=True))
                    am_ap = am2_col[:]
                qst["wt_amx"] = t_amx
                qst["wt_amx%d" % par] = t_amx
                W(nc.vector, t_amx, qst.get("sj_rd"))
                t_rc = VE.do(nc.vector.reciprocal(sj_col[:], am_ap))
                W(nc.vector, t_rc)
                t_sj = VE.do(nc.vector.tensor_scalar_mul(sj_col[:], sj_col[:], 127.0))
                W(nc.scalar, t_sj, t_l, t_mg, qst.get("wqf_rd"))
                t_ri = SC.do(nc.scalar.activation(wq_f[:], wtile[:], AF.Identity,
                                                  scale=sj_col[:, 0:1], bias=magic_col[:, 0:1]))
                qst["wt_ri%d" % par] = t_ri
                qst["sj_rd"] = t_ri
                if kind == 1:
                    W(nc.vector, t_ri, qst.get("wqbf_rd"))
                    t_q = VE.do(nc.vector.tensor_scalar_add(wq_bf[:], wq_f[:], -MAGIC))
                    qst["wqf_rd"] = t_q
                    evtoks = []
                    for k in range(PB):
                        half = k % 2
                        W(nc.tensor, t_q, t_id)
                        if k >= 2:
                            W(nc.tensor, evtoks[k - 2])
                        t_tr = TE.do(nc.tensor.transpose(
                            ps_tr[half][:], wq_bf[:, k * 128:(k + 1) * 128], ident[:]))
                        W(nc.scalar, t_tr, qst.get("wqbf2_rd"))
                        evtoks.append(SC.do(nc.scalar.copy(
                            wq_bf2[:, k * 128:(k + 1) * 128], ps_tr[half][:])))
                    qst["wqbf_rd"] = t_tr
                    W(GQ, *evtoks)
                    t_w = sm_wqw[par].do(GQ.dma_start(
                        out=dst_dram[:, jt * D:(jt + 1) * D], in_=wq_bf2[:]))
                    qst["wqbf2_rd"] = t_w
                    qst["w1fin%d" % par] = t_w
                    w1q_tok[jt] = t_w
                else:
                    W(nc.vector, t_ri)
                    t_iv = VE.do(nc.vector.tensor_scalar_mul(sj_col[:], am_ap, 1.0 / 127.0))
                    qst["am2_rd"] = t_iv
                    qst["sj_rd"] = t_iv
                    W(nc.vector, t_iv, qst.get("wqbf_rd"))
                    t_q = VE.do(nc.vector.tensor_scalar(wq_f[:], wq_f[:], -MAGIC,
                                                        sj_col[:, 0:1], ALU.add, ALU.mult))
                    qst["wqf_rd"] = t_q
                    W(nc.vector, t_q, qst.get("wqbf_rd"))
                    t_c = VE.do(nc.vector.tensor_copy(wq_bf[:], wq_f[:]))
                    W(GQ, t_c, qst.get("w1fin%d" % par))
                    t_w = sm_wqw[par].do(GQ.dma_start(
                        out=dst_dram[:, jt * D:(jt + 1) * D], in_=wq_bf[:]))
                    qst["wqbf_rd"] = t_w
                    w2q_tok[jt] = t_w

            def mm1_qx(tc1):
                nonlocal t_qxp, t_qx_free
                c1 = tc1 * TC1
                toks = {}
                for k in range(PB):
                    W(nc.scalar, t_sc1, (ccx1, 1), t_mg, t_mgn, t_qxp, t_qx_free)
                    t_r1 = SC.do(nc.scalar.activation(
                        qx_f[:], xbf[:, k * TB + c1:k * TB + c1 + TC1], AF.Identity,
                        scale=sc1[:, 0:1], bias=magic_col[:, 0:1]))
                    W(nc.scalar, t_r1)
                    toks[k] = SC.do(nc.scalar.activation(
                        qx_ch[:, k * TC1:(k + 1) * TC1], qx_f[:], AF.Identity,
                        bias=magic_neg[:, 0:1]))
                    t_qxp = toks[k]
                return toks

            def mm1_job(tc1, m, t_qx):
                nonlocal t_qx_free, t_epi
                c1 = tc1 * TC1
                mpar = m % 2
                m4 = m % 3
                W(SY, t_w1s_rel[m4], w1q_tok[m])
                t_wl = sm_w1r[m % 4].do(SY.dma_start(
                    out=w1s[m4][:], in_=w1q_dram[:, m * D:(m + 1) * D]))
                bank = pm[mpar]
                W(nc.tensor, t_wl, pm_ev[mpar], *t_qx.values())
                for k in range(PB):
                    mm = nc.tensor.matmul(bank[:, :], w1s[m4][:, k * 128:(k + 1) * 128],
                                          qx_ch[:, k * TC1:(k + 1) * TC1],
                                          start=(k == 0), stop=(k == PB - 1))
                t_mmf = TE.do(mm)
                t_w1s_rel[m4] = t_mmf
                t_qx_free = t_mmf
                W(nc.scalar, t_mmf, t_hid_rel[mpar], t_epi)
                t_hev = SC.do(nc.scalar.activation(
                    hid_sb[mpar][:], bank[:, :], AF.Gelu,
                    scale=epi1[:, m:m + 1], bias=binv[:, m:m + 1]))
                pm_ev[mpar] = t_hev
                W(SQ, t_hev)
                t_hw = sm_hidw[mpar].do(SQ.dma_start(
                    out=hid_dram[:, m * TB + c1:m * TB + c1 + TC1], in_=hid_sb[mpar][:]))
                t_hid_rel[mpar] = t_hw

            prep_tok = {}
            t_hidw_q = [None, None]

            def qh_prep_unit(j):
                nonlocal t_qhf_rd
                tc2, k = divmod(j, NJT)
                u = k * NT1 + tc2
                kpar = j % 4
                wpar = j % 2
                W(SY, t_qh_rel[kpar], t_hidw_q[0], t_hidw_q[1])
                t_hl = sm_hidr[kpar].do(SY.dma_start(
                    out=qh_t[kpar][:], in_=hid_dram[:, u * TC1:(u + 1) * TC1]))
                W(nc.scalar, t_hl, t_sc2, (ccx2, 1), t_mg, t_mgn, t_qhf_rd, t_qxp, t_qx_free)
                t_r1 = SC.do(nc.scalar.activation(
                    qx_f[:], qh_t[kpar][:], AF.Identity,
                    scale=sc2[:, 0:1], bias=magic_col[:, 0:1]))
                W(nc.scalar, t_r1, t_qhw.get(wpar))
                t_qh = SC.do(nc.scalar.activation(
                    hid_sb[wpar][:], qx_f[:], AF.Identity, bias=magic_neg[:, 0:1]))
                t_qhf_rd = t_qh
                W(SQ, t_qh)
                t_qhw[wpar] = sm_qhw[wpar].do(SQ.dma_start(
                    out=qh_dram[:, u * TC1:(u + 1) * TC1], in_=hid_sb[wpar][:]))
                prep_tok[(k, tc2)] = t_qhw[wpar]
                t_qh_rel[kpar] = t_r1

            def mm2_unit(u):
                halfm, rem = divmod(u, NT1 * NJT)
                tc2, k = divmod(rem, NJT)
                c1 = tc2 * TC1
                kpar = k % 4
                banks4 = [pp[0], pp[1], pm[0], pm[1]]
                W(SY, t_w2s_rel[kpar], w2q_tok[k])
                t_w2l = sm_w2r[kpar].do(SY.dma_start(
                    out=w2s[kpar][:, 0:512],
                    in_=w2q_dram[:, k * D + halfm * 512:k * D + halfm * 512 + 512]))
                W(SY, t_qh_rel[kpar], prep_tok[(k, tc2)])
                t_hl = sm_hidr[kpar].do(SY.dma_start(
                    out=qh_t[kpar][:], in_=qh_dram[:, (k * NT1 + tc2) * TC1:(k * NT1 + tc2 + 1) * TC1]))
                t_qh = t_hl
                W(nc.tensor, t_w2l, t_qh)
                if k == 0:
                    W(nc.tensor, *[mm2_ev.get(i) for i in range(4)], *bank_ev, *pm_ev)
                t_mm_last = None
                for mi in range(4):
                    m = halfm * 4 + mi
                    mm = nc.tensor.matmul(
                        banks4[mi][:, :], w2s[kpar][:, mi * 128:(mi + 1) * 128],
                        qh_t[kpar][:], start=(k == 0), stop=(k == NJT - 1))
                t_mm_last = TE.do(mm)
                t_w2s_rel[kpar] = t_mm_last
                t_qh_rel[kpar] = t_mm_last
                if k == NJT - 1:
                    for mi in range(4):
                        m = halfm * 4 + mi
                        fp = m % 2
                        W(nc.scalar, t_mm_last, t_fw_prev[fp])
                        t_fe = SC.do(nc.scalar.activation(
                            ffn_sb[fp][:], banks4[mi][:, :], AF.Identity,
                            scale=sc2[:, 1:2], bias=boutv[:, m:m + 1]))
                        mm2_ev[mi] = t_fe
                        W(SQ, t_fe)
                        t_fw_prev[fp] = sm_hidw[fp].do(SQ.dma_start(
                            out=ffn_dram[:, m * TB + c1:m * TB + c1 + TC1],
                            in_=ffn_sb[fp][:]))
                        t_ffnw[(tc2, m)] = t_fw_prev[fp]

            for jt_ in range(32):
                quant_tile(1, jt_)
            load_xnf(0)
            for job in prep_jobs(0):
                job()

            quant_sched = {2: [(2, j) for j in range(16)],
                           3: [(2, j) for j in range(16, 32)]}
            MM2_CH = NCH - 1
            N_MM2 = 2 * NT1 * NJT

            for ch in range(NCH):
                par = ch % 2
                c0g = ch * TCS
                if ch == NCH // 2:
                    W(GQ, sm_yt[0].tok(), sm_yt[1].tok())
                    GQ.collective_compute("AllGather", ALU.bypass, ins=[ag3_in[0]],
                                          outs=[ag3_out[0]],
                                          replica_groups=GRP4).then_inc(cc3[0], 1)
                if ch + 1 < NCH:
                    load_xnf(ch + 1)
                jobs = prep_jobs(ch + 1) if ch + 1 < NCH else []
                if 1 <= ch < 1 + NT1:
                    tc1 = ch - 1
                    qx_box = {}

                    def qx_job(tc1=tc1, box=qx_box):
                        box.update(mm1_qx(tc1))

                    mm1jobs = [qx_job] + [
                        (lambda m=m, tc1=tc1, box=qx_box: mm1_job(tc1, m, box))
                        for m in range(NJT)]
                    mixed = []
                    ia = ib = 0
                    while ia < len(jobs) or ib < len(mm1jobs):
                        if ia < len(jobs):
                            mixed.append(jobs[ia]); ia += 1
                        for _ in range(3):
                            if ib < len(mm1jobs):
                                mixed.append(mm1jobs[ib]); ib += 1
                    jobs = mixed
                jobs_at = {}
                for j, job in enumerate(jobs):
                    jobs_at.setdefault(1 + j, []).append(job)
                if ch == 1:
                    W(SY, (ccx1, 1))
                    t_agr = sm_ax.do(SY.dma_start(out=agrow[:], in_=ax1_out.rearrange("r one -> one r")))
                    W(nc.vector, t_agr)
                    t_g0 = VE.do(nc.vector.reduce_max(rowtmp[0:1, 1:2], agrow[:], AX))
                    W(nc.vector, t_g0)
                    t_g1 = VE.do(nc.vector.reciprocal(rowtmp[0:1, 2:3], rowtmp[0:1, 1:2]))
                    W(nc.vector, t_g1)
                    t_g2 = VE.do(nc.vector.tensor_scalar_mul(rowtmp[0:1, 2:3], rowtmp[0:1, 2:3], 32767.0))
                    t_g3 = VE.do(nc.vector.tensor_scalar_mul(rowtmp[0:1, 3:4], rowtmp[0:1, 1:2], 1.0 / 32767.0))
                    W(SY, t_g2, t_g3)
                    t_sw = sm_ax.do(SY.dma_start(out=scal_dram[0:1, 0:2], in_=rowtmp[0:1, 2:4]))
                    W(SY, t_sw)
                    t_sc1 = sm_ax.do(SY.dma_start(out=sc1[:], in_=scal_dram[0:1, 0:2].broadcast_to([P, 2])))
                    W(nc.vector, t_sc1, (ccx1, 1), qst["wt_amx"])
                    t_epi = VE.do(nc.vector.tensor_scalar(epi1[:], am_all[:], sc1[:, 1:2],
                                                          1.0 / 127.0, ALU.mult, ALU.mult))
                if ch == 6:
                    W(GQ, sm_yt[0].tok(), sm_yt[1].tok())
                    GQ.collective_compute("AllGather", ALU.bypass, ins=[ag3_in[1]],
                                          outs=[ag3_out[1]],
                                          replica_groups=GRP4).then_inc(cc3[2], 1)
                if ch == 7:
                    W(GQ, sm_yt[0].tok(), sm_yt[1].tok())
                    GQ.collective_compute("AllGather", ALU.bypass, ins=[ag3_in[2]],
                                          outs=[ag3_out[2]],
                                          replica_groups=GRP4).then_inc(cc3[3], 1)
                if ch == 6:
                    W(SY, (ccx2, 1))
                    t_agr2 = sm_ax2.do(SY.dma_start(out=agrow[:], in_=ax2_out.rearrange("r one -> one r")))
                    W(nc.vector, t_agr2)
                    t_b1 = VE.do(nc.vector.reduce_max(rowtmp[0:1, 5:6], agrow[:], AX))
                    W(nc.vector, t_b1)
                    t_b2 = VE.do(nc.vector.reciprocal(rowtmp[0:1, 6:7], rowtmp[0:1, 5:6]))
                    W(nc.vector, t_b2)
                    t_b3 = VE.do(nc.vector.tensor_scalar_mul(rowtmp[0:1, 6:7], rowtmp[0:1, 6:7], 32767.0))
                    t_b4 = VE.do(nc.vector.tensor_scalar_mul(rowtmp[0:1, 7:8], rowtmp[0:1, 5:6], 1.0 / 32767.0))
                    W(SY, t_b3, t_b4)
                    t_sw2 = sm_ax2.do(SY.dma_start(out=scal2_dram[0:1, 0:2], in_=rowtmp[0:1, 6:8]))
                    W(SY, t_sw2)
                    t_sc2 = sm_ax2.do(SY.dma_start(out=sc2[:], in_=scal2_dram[0:1, 0:2].broadcast_to([P, 2])))
                    if debug:
                        W(SY, t_sc2)
                        sm_dbg.do(SY.dma_start(out=dbg["dbg_sc2"], in_=sc2[:]))
                if ch == MM2_CH - 1 and not q67_jobs:
                    seq = []
                    for tc2 in range(NT1):
                        seq += [("p", tc2 * NJT + k) for k in range(NJT)]
                        if tc2 >= 1:
                            seq += [("m", (tc2 - 1) * NJT + k) for k in range(NJT)]
                    seq += [("m", 3 * NJT + k) for k in range(NJT)]
                    seq += [("m", NT1 * NJT + uu) for uu in range(NT1 * NJT)]
                    q67_jobs.extend(seq)

                t_rep_tok = {}
                t_aa_tok = {}

                def issue_rep(nn):
                    q = nn % 4
                    W(SY, t_rep_rel[q], C_bcw[ch])
                    sm_rep[q].do(SY.dma_start(
                        out=brep[q][:],
                        in_=bc_stage[nn:nn + 1, c0g:c0g + TCS].broadcast_to([P, TCS])))
                    t_rep_tok[q] = sm_rep[q].do(SY.dma_start(
                        out=crep[q][:],
                        in_=bc_stage[32 + nn:33 + nn, c0g:c0g + TCS].broadcast_to([P, TCS])))

                def issue_exp(nn):
                    q = nn % 4
                    for dc_ in range(NDC):
                        dsl_ = slice(dc_ * TCS, (dc_ + 1) * TCS)
                        W(nc.scalar, C_d_ev[(ch, dc_, 0)], C_d_ev[(ch, dc_, 1)],
                          t_scan_last.get((q, dc_)))
                        t_aa_tok[(q, dc_)] = SC.do(nc.scalar.activation(
                            a_b[q][:, dsl_], dlt[par][:, dsl_], AF.Exp,
                            scale=A_s[:, dc_ * N + nn:dc_ * N + nn + 1]))
                        if nn == N - 1:
                            t_dlt_free[(par, dc_)] = t_aa_tok[(q, dc_)]

                issue_rep(0)
                issue_rep(1)
                issue_exp(0)
                issue_exp(1)
                for n in range(N):
                    q4 = n % 4
                    if n + 2 < N:
                        issue_rep(n + 2)
                        issue_exp(n + 2)
                    brep_v = brep[q4][:].rearrange("p (o t) -> p o t", o=1).broadcast_to([P, NDC, TCS])
                    crep_v = crep[q4][:].rearrange("p (o t) -> p o t", o=1).broadcast_to([P, NDC, TCS])
                    bb_v = b_b[:].rearrange("p (c t) -> p c t", c=NDC)
                    hb_v = h_b[:].rearrange("p (c t) -> p c t", c=NDC)
                    yb_v = y_b[:].rearrange("p (c t) -> p c t", c=NDC)
                    ub_v = u_bf[par][:].rearrange("p (c t) -> p c t", c=NDC)
                    W(nc.vector, t_rep_tok[q4], C_u_done[(ch, 0)], C_u_done[(ch, 1)],
                      t_scan_last.get((q4, 0)), t_scan_last.get((q4, 1)),
                      t_scan_last.get(((n - 1) % 4, 0)), t_scan_last.get(((n - 1) % 4, 1)),
                      t_h_free.get(0), t_h_free.get(1))
                    t_bb = VE.do(nc.vector.tensor_tensor(bb_v, brep_v, ub_v, ALU.mult))
                    for dc in range(NDC):
                        ci = n * NDC + dc
                        dsl = slice(dc * TCS, (dc + 1) * TCS)
                        t_aa = t_aa_tok[(q4, dc)]
                        W(nc.vector, t_aa, t_bb, t_c0,
                          t_h_free.get(dc), t_carry_tok.get((n, dc)))
                        init = 0.0 if ch == 0 else carry[:, ci:ci + 1]
                        t_h = VE.do(nc.vector.tensor_tensor_scan(
                            h_b[:, dsl], a_b[q4][:, dsl], b_b[:, dsl],
                            init, ALU.mult, ALU.add))
                        t_scan_last[(q4, dc)] = t_h
                    W(nc.vector, t_scan_last[(q4, 0)], t_scan_last[(q4, 1)])
                    t_cc = VE.do(nc.vector.tensor_copy(
                        carry[:, 2 * n:2 * n + 2].rearrange("p (c o) -> p c o", c=NDC),
                        hb_v[:, :, TCS - 1:TCS]))
                    t_carry_tok[(n, 0)] = t_cc
                    t_carry_tok[(n, 1)] = t_cc
                    if n == 0:
                        W(nc.vector, t_yb_free.get(0), t_yb_free.get(1), t_yt_prev)
                        t_y = VE.do(nc.vector.tensor_tensor(yb_v, crep_v, hb_v, ALU.mult))
                    else:
                        t_hc = VE.do(nc.vector.tensor_tensor(bb_v, crep_v, hb_v, ALU.mult))
                        W(nc.vector, t_hc)
                        t_y = VE.do(nc.vector.tensor_add(y_b[:], y_b[:], b_b[:]))
                    t_h_free[0] = t_y
                    t_h_free[1] = t_y
                    t_rep_rel[q4] = t_y
                    if ch >= MM2_CH - 1:
                        it67 = (ch - (MM2_CH - 1)) * 2 * N + n * NDC
                        budget = 8 if it67 >= 18 else (4 if ch == MM2_CH - 1 else 8)
                        while budget > 0 and q67_jobs:
                            knd, uu = q67_jobs.pop(0)
                            (qh_prep_unit if knd == "p" else mm2_unit)(uu)
                            budget -= 1
                    for job in jobs_at.pop(n, []):
                        job()
                for nn in sorted(jobs_at):
                    for job in jobs_at[nn]:
                        job()
                while ch == MM2_CH and q67_jobs:
                    knd, uu = q67_jobs.pop(0)
                    (qh_prep_unit if knd == "p" else mm2_unit)(uu)

                for dc in range(NDC):
                    dsl = slice(dc * TCS, (dc + 1) * TCS)
                    W(nc.vector, t_h_free[dc], t_yt_prev)
                    t_ytd = VE.do(nc.vector.scalar_tensor_tensor(
                        y_b[:, dsl], u_bf[par][:, dsl],
                        dparam[:, dc:dc + 1], y_b[:, dsl], ALU.mult, ALU.add))
                    t_u_free[(par, dc)] = t_ytd
                    t_yb_free[dc] = t_ytd
                    if debug:
                        W(SY, t_ytd)
                        sm_dbg.do(SY.dma_start(out=dbg["dbg_y"][:, dc * ST + c0g:dc * ST + c0g + TCS],
                                               in_=y_b[:, dsl]))
                    W(GQ, t_ytd)
                    hh3 = 0 if ch < 4 else (1 if ch < 6 else (2 if ch == 6 else 3))
                    w3 = AG3W[hh3]
                    off3 = c0g - AG3S[hh3]
                    sm_yt[par].do(GQ.dma_start(
                        out=ag3_in[hh3][:, dc * w3 + off3:dc * w3 + off3 + TCS],
                        in_=y_b[:, dsl]))
                t_yt_prev = sm_yt[par].tok()
                if debug:
                    W(SY, C_d_ev[(ch, 0, 1)], C_d_ev[(ch, 1, 1)], C_d_ev[(ch, 0, 0)], C_d_ev[(ch, 1, 0)])
                    for dc in range(NDC):
                        sm_dbg.do(SY.dma_start(out=dbg["dbg_delta"][:, dc * ST + c0g:dc * ST + c0g + TCS],
                                               in_=dlt[par][:, dc * TCS:(dc + 1) * TCS]))
                        sm_dbg.do(SY.dma_start(out=dbg["dbg_u"][:, dc * ST + c0g:dc * ST + c0g + TCS],
                                               in_=u_pad[:, dc * UPS + 3:dc * UPS + 3 + TCS]))

                for kind, jt in quant_sched.get(ch, []):
                    quant_tile(kind, jt)

                if ch == 4:
                    t_hh = [sm_hidw[0].tok(), sm_hidw[1].tok()]
                    t_hidw_q[0], t_hidw_q[1] = t_hh
                    t_hx = None
                    for mi in range(NJT):
                        hpar = mi % 2
                        W(SY, *t_hh)
                        if mi >= 2:
                            W(SY, t_hx)
                        if mi < 2:
                            W(SY, t_qx_free)
                        t_hl = sm_hidr[mi % 2].do(SY.dma_start(
                            out=ham_t[hpar], in_=hid_dram[:, mi * TB:(mi + 1) * TB]))
                        W(nc.vector, t_hl)
                        if mi == 0:
                            t_hx = VE.do(nc.vector.reduce_max(hamax_acc[:], ham_t[hpar], AX,
                                                              apply_absolute_value=True))
                        else:
                            t_q = VE.do(nc.vector.reduce_max(hamax_tmp[:], ham_t[hpar], AX,
                                                             apply_absolute_value=True))
                            W(nc.vector, t_q)
                            t_hx = VE.do(nc.vector.tensor_max(hamax_acc[:], hamax_acc[:], hamax_tmp[:]))
                    W(GQ, t_hx)
                    t_aw2 = sm_ax2g.do(GQ.dma_start(out=amax_dram, in_=hamax_acc[:]))
                    W(SY, t_aw2)
                    t_ar2 = sm_ax2.do(SY.dma_start(out=arow[:], in_=amax_dram.rearrange("p one -> one p")))
                    W(nc.vector, t_ar2)
                    t_b0 = VE.do(nc.vector.reduce_max(rowtmp[0:1, 4:5], arow[:], AX))
                    W(GQ, t_b0)
                    t_axw2 = sm_ax2g.do(GQ.dma_start(out=ax2_in, in_=rowtmp[0:1, 4:5]))
                    W(GQ, t_axw2)
                    GQ.collective_compute("AllGather", ALU.bypass, ins=[ax2_in], outs=[ax2_out],
                                          replica_groups=GRP8).then_inc(ccx2, 1)


            W(GQ, sm_yt[0].tok(), sm_yt[1].tok())
            GQ.collective_compute("AllGather", ALU.bypass, ins=[ag3_in[3]], outs=[ag3_out[3]],
                                  replica_groups=GRP4).then_inc(cc3[1], 1)

        if _TRUNC == 2:
            W(SY, (VE.sem, VE.n), (SC.sem, SC.n), (TE.sem, TE.n))
            smd = DSem(nc, "sm_trunc")
            for mm_ in range(PB):
                smd.do(SY.dma_start(out=out_fm[:, mm_ * TB:(mm_ + 1) * TB],
                                    in_=x_p[:, mm_ * TB:(mm_ + 1) * TB]))
            W(SY, smd.tok())
            return nc

        with contextlib.ExitStack() as TL:
            def sbt(name, shape, dt=F32):
                return TL.enter_context(nc.sbuf_tensor(name, shape, dt))

            pq = [TL.enter_context(nc.psum_tensor(f"pq{i}", [P, TC1], F32)) for i in range(8)]
            ffn2_t = [sbt(f"ffn2_t{i}", [P, TC1], BF16) for i in range(2)]
            woutT_bf = sbt("woutT_bf", [P, PB * D], BF16)
            wostage = sbt("wostage", [P, PB * D])
            y_own = sbt("y_own", [P, PB * TB], BF16)
            x2_t = [sbt(f"x2_t{i}", [P, TC1]) for i in range(2)]
            at_t = [sbt(f"at_t{i}", [P, TC1]) for i in range(2)]
            o_t = [sbt(f"o_t{i}", [P, TC1]) for i in range(2)]

            for qq in (nc.vector, nc.scalar, nc.tensor, GQ, SY):
                W(qq, (SC.sem, SC.n), (VE.sem, VE.n), (TE.sem, TE.n),
                  sm_yt[0].tok(), sm_yt[1].tok(), sm_xnf[0].tok(), sm_xnf[1].tok(),
                  sm_rep[0].tok(), sm_rep[1].tok(), sm_bcw[0].tok(), sm_bcw[1].tok(),
                  sm_hidw[0].tok(), sm_hidw[1].tok(), sm_qhw[0].tok(), sm_qhw[1].tok(),
                  sm_hidr[0].tok(), sm_hidr[1].tok(),
                  sm_w1r[0].tok(), sm_w1r[1].tok(), sm_wqw[0].tok(), sm_wqw[1].tok(),
                  sm_w1[0].tok(), sm_w1[1].tok())
            t_wo = sm_ld.do(SY.dma_start(out=wostage[:], in_=woutT_p))
            W(nc.vector, t_wo)
            t_wob = VE.do(nc.vector.tensor_copy(woutT_bf[:], wostage[:]))

            pq_ev = [None] * 8
            if debug:
                W(SY, *[t_ffnw[(t2, m2)] for t2 in range(NT1) for m2 in range(PB)])
                sm_dbg.do(SY.dma_start(out=dbg["dbg_ffn"], in_=ffn_dram))

            HB2 = TB // 2
            for arm in SY.switch_core_id(n=128):
                blk = arm.logical % 4
                if blk < 2:
                    plan = [(0, ST // 2, blk * TB, cc3[0]), (0, ST // 2, blk * TB + HB2, cc3[0])]
                elif blk == 2:
                    plan = [(1, ST // 4, 0, cc3[2]), (1, ST // 4, HB2, cc3[2])]
                else:
                    plan = [(2, ST // 8, 0, cc3[3]), (3, ST // 8, 0, cc3[1])]
                for hf, (h3, w3, off3, ccw) in enumerate(plan):
                    sems = sm_yog if hf == 0 else sm_yog2
                    SY.wait_ge(ccw, 1)
                    for r in range(4):
                        for c in range(NDC):
                            g = r * NDC + c
                            SY.dma_start(
                                out=y_own[:, g * TB + hf * HB2:g * TB + hf * HB2 + HB2],
                                in_=ag3_out[h3][r, :, c * w3 + off3:c * w3 + off3 + HB2]
                            ).then_inc(sems[g].sem, 16)
            for g in range(PB):
                sm_yog[g].n += 16
                sm_yog2[g].n += 16
            t_yo_h = [[sm_yog[g].tok() for g in range(PB)],
                      [sm_yog2[g].tok() for g in range(PB)]]

            t_o2_prev = [None, None]
            t_st_prev = [None, None]
            it = 0
            for tc in range(NT1):
                c1 = tc * TC1
                for m in range(PB):
                    ip = it % 2
                    bank = pq[m]
                    hf = tc // 2
                    W(nc.tensor, t_wob, pq_ev[m])
                    if m >= 2:
                        W(nc.tensor, *t_yo_h[hf])
                    for k in range(PB):
                        if m < 2:
                            W(nc.tensor, t_yo_h[hf][k])
                        mm = nc.tensor.matmul(bank[:, :],
                                              woutT_bf[:, k * D + m * 128:k * D + m * 128 + 128],
                                              y_own[:, k * TB + c1:k * TB + c1 + TC1],
                                              start=(k == 0), stop=(k == PB - 1))
                    t_mm_o = TE.do(mm)
                    W(SY, t_o2_prev[ip])
                    t_xl2 = sm_x2[ip].do(SY.dma_start(
                        out=x2_t[ip][:], in_=x_p[:, m * TB + c1:m * TB + c1 + TC1]))
                    W(SY, t_ffnw[(tc, m)], t_o2_prev[ip])
                    t_fl2 = sm_ffnr[ip].do(SY.dma_start(
                        out=ffn2_t[ip][:], in_=ffn_dram[:, m * TB + c1:m * TB + c1 + TC1]))
                    W(nc.scalar, t_mm_o, t_o2_prev[ip])
                    t_ae = SC.do(nc.scalar.copy(at_t[ip][:], bank[:, :]))
                    pq_ev[m] = t_ae
                    W(nc.vector, t_ae, t_xl2, t_fl2, t_st_prev[ip])
                    t_o1 = VE.do(nc.vector.tensor_add(o_t[ip][:], at_t[ip][:], x2_t[ip][:]))
                    W(nc.vector, t_o1)
                    t_o2 = VE.do(nc.vector.tensor_add(o_t[ip][:], o_t[ip][:], ffn2_t[ip][:]))
                    t_o2_prev[ip] = t_o2
                    W(SY, t_o2)
                    t_st_prev[ip] = sm_out[ip].do(SY.dma_start(
                        out=out_fm[:, m * TB + c1:m * TB + c1 + TC1], in_=o_t[ip][:]))
                    it += 1

            W(SY, sm_out[0].tok(), sm_out[1].tok())
            if debug:
                W(SY, sm_dbg.tok())

    return nc


def _fm(a):
    r = a.shape[0] // P
    return np.ascontiguousarray(a.reshape(r, P, a.shape[1]).transpose(1, 0, 2).reshape(P, -1))


def kernel(**inputs):
    x = np.asarray(inputs["L0_attn_x"])
    W_in = np.asarray(inputs["L0_attn_W_in"])
    W_out = np.asarray(inputs["L0_attn_W_out"])
    conv_w = np.asarray(inputs["L0_attn_conv_w"])
    conv_b = np.asarray(inputs["L0_attn_conv_b"])
    A_log = np.asarray(inputs["L0_attn_A_log"])
    D_param = np.asarray(inputs["L0_attn_D_param"])
    Wf1 = np.asarray(inputs["L0_ffn_W_in"])
    bf1 = np.asarray(inputs["L0_ffn_b_in"])
    Wf2 = np.asarray(inputs["L0_ffn_W_out"])
    bf2 = np.asarray(inputs["L0_ffn_b_out"])

    x_fm = np.ascontiguousarray(np.transpose(x, (0, 2, 1)))
    woutT = _fm(np.ascontiguousarray(W_out.T))
    wffn1T = _fm(np.ascontiguousarray(Wf1.T))
    wffn2 = _fm(np.ascontiguousarray(Wf2))
    bin_p = np.ascontiguousarray(bf1.reshape(NJT, P).T)
    bout_p = np.ascontiguousarray(bf2.reshape(PB, P).T)

    in_maps = []
    for c in range(8):
        s, q = c // 4, c % 4
        sl = slice(DS * q, DS * (q + 1))
        wproj = np.concatenate([W_in[DS * q:DS * (q + 1)],
                                W_in[D + DS * q:D + DS * (q + 1)],
                                W_in[2 * D:2 * D + 64]], axis=0)
        in_maps.append({
            "x_p": _fm(np.ascontiguousarray(x_fm[s, :, TB * q:TB * (q + 1)])),
            "wprojT_p": _fm(np.ascontiguousarray(wproj.T)),
            "woutT_p": woutT,
            "wffn1T_p": wffn1T,
            "wffn2_p": wffn2,
            "convw_p": _fm(np.ascontiguousarray(conv_w[sl, 0, :])),
            "convb_p": _fm(conv_b[sl].reshape(DS, 1)),
            "dparam_p": _fm(D_param[sl].reshape(DS, 1)),
            "alogT_p": _fm(np.ascontiguousarray(A_log.T[sl])),
            "bin_p": bin_p,
            "bout_p": bout_p,
        })

    nc = build(debug=False)
    res = run_bass_kernel_spmd(nc, in_maps, core_ids=list(range(8)))

    out = np.empty((2, ST, D), np.float32)
    for c in range(8):
        s, q = c // 4, c % 4
        o = np.asarray(res.results[c]["out_fm"])
        o = o.reshape(P, PB, TB).transpose(1, 0, 2).reshape(D, TB)
        out[s, TB * q:TB * (q + 1), :] = o.T
    return out
